# version 10
# baseline (speedup 1.0000x reference)
import numpy as np
import concourse.bass as bass
import concourse.mybir as mybir
from concourse.bass_utils import run_bass_kernel_spmd
from contextlib import ExitStack

F32 = mybir.dt.float32
BF16 = mybir.dt.bfloat16
FP8 = mybir.dt.float8e4
AF = mybir.ActivationFunctionType
ALU = mybir.AluOpType
AX = mybir.AxisListType

D = 1024
L = 4096
H = 16
KC = 8
JC = 32
LT_A, NTA, LCA = 512, 8, 4
LT_B, NTB = 256, 16

B8 = 25.132740020751953
C1X8 = 50.25
C2X8 = float(np.float32(16 * np.pi - 50.25))
EPS_LN = float(np.float32(1e-5))
EPS_G2 = float(np.float32(5e-7))
INV2_23 = float(2.0 ** -23)


class SemC:
    __slots__ = ("h", "n")

    def __init__(self, h):
        self.h = h
        self.n = 0

    def bump(self, instr, k=1):
        instr.then_inc(self.h, k)
        self.n += k
        return self.n


_WAITED = {}


class Emitter:

    ORDER = ("tensor", "scalar", "vector", "sync", "gpsimd")

    def __init__(self, block):
        self.block = block
        self.q = []

    def _add(self, name, f):
        self.q.append((name, f))

    def tensor(self, f):
        self._add("tensor", f)

    def scalar(self, f):
        self._add("scalar", f)

    def vector(self, f):
        self._add("vector", f)

    def sync(self, f):
        self._add("sync", f)

    def gpsimd(self, f):
        self._add("gpsimd", f)

    def flush(self):
        for name in self.ORDER:
            fns = [f for e, f in self.q if e == name]
            if fns:
                getattr(self.block, name)(lambda E, fns=fns: [f(E) for f in fns])
        self.q = []


def _wk(eng, sem, ms, key):
    _w(eng, sem, ms.get(key, 0))


def _w(eng, sem, val):
    if not val or val <= 0:
        return
    k = (type(eng).__name__, id(sem))
    if _WAITED.get(k, 0) >= val:
        return
    _WAITED[k] = val
    eng.wait_ge(sem.h, val)


def build_nc(phases='full', ms_preset=None, _ms_out=None, _two_pass=True):
    if _two_pass:
        holder = []
        build_nc(phases, ms_preset=None, _ms_out=holder, _two_pass=False)
        return build_nc(phases, ms_preset=holder[0], _ms_out=None, _two_pass=False)
    _WAITED.clear()
    nc = bass.Bass("TRN2", target_bir_lowering=False, detect_race_conditions=False)

    def din(name, shape):
        return nc.dram_tensor(name, shape, F32, kind="ExternalInput").ap()

    x_d = din("xT", [D, L])
    wq_d = din("wqT", [D, D])
    wk_d = din("wkT", [D, D])
    wv_d = din("wvT", [D, D])
    wo_d = din("woT", [D, D])
    wfc_d = din("wfcT", [D, 4096])
    wpj_d = din("wpjT", [4096, D])
    projz_d = din("projz", [128, 256])
    eye_d = din("eye", [128, 128])
    ones_d = din("ones", [128, 128])
    bqc_d = din("bqc", [128, 8])
    bkc_d = din("bkc", [128, 8])
    boc_d = din("boc", [128, 8])
    bvb_d = din("bvb", [128, D])
    lnw_d = din("lnwc", [128, 32])
    lnb_d = din("lnbc", [128, 32])
    out_d = nc.dram_tensor("outT", [D, L], F32, kind="ExternalOutput").ap()

    perm = ExitStack()

    def sb(name, shape, dt):
        return perm.enter_context(nc.sbuf_tensor(name, shape, dt))

    projz_b = sb("projz_b", [128, 256], BF16)
    eye_b = sb("eye_b", [128, 128], BF16)
    ones_f = sb("ones_f", [128, 128], F32)
    ones_b = sb("ones_b", [128, 128], BF16)
    bqc = sb("bqc_s", [128, 8], F32)
    bkc = sb("bkc_s", [128, 8], F32)
    boc = sb("boc_s", [128, 8], F32)
    bvb = sb("bvb_s", [128, D], F32)
    lnwf = sb("lnwf", [128, 32], F32)
    wcol_b = sb("wcol_b", [128, 32], BF16)
    bcol_b = sb("bcol_b", [128, 32], BF16)
    eps5 = sb("eps5", [128, 1], F32)
    eps57 = sb("eps57", [128, 1], F32)
    kv_acc = sb("kv_acc", [128, 2048], F32)

    ps = [perm.enter_context(nc.psum_tensor(f"ps{i}", [128, 512], F32)) for i in range(6)]
    psS = perm.enter_context(nc.psum_tensor("psS", [128, 512], F32))

    sDL = SemC(perm.enter_context(nc.semaphore("sDL")))
    sDG = SemC(perm.enter_context(nc.semaphore("sDG")))
    sXA = [SemC(perm.enter_context(nc.semaphore(f"sXA{i}"))) for i in range(2)]
    sXF = [SemC(perm.enter_context(nc.semaphore(f"sXF{i}"))) for i in range(2)]
    sXB = [SemC(perm.enter_context(nc.semaphore(f"sXB{i}"))) for i in range(2)]
    sWS = [SemC(perm.enter_context(nc.semaphore(f"sWS{i}"))) for i in range(2)]
    sT = SemC(perm.enter_context(nc.semaphore("sT")))
    sS = SemC(perm.enter_context(nc.semaphore("sS")))
    sV = SemC(perm.enter_context(nc.semaphore("sV")))
    sO = SemC(perm.enter_context(nc.semaphore("sO")))

    ms = dict(ms_preset) if ms_preset else {}
    if _ms_out is not None:
        _ms_out.append(ms)

    with nc.Block() as block:
        em = Emitter(block)
        @em.sync
        def _(q):
            for dst, src in ((ones_f, ones_d), (bqc, bqc_d), (bkc, bkc_d),
                             (boc, boc_d), (bvb, bvb_d), (lnwf, lnw_d)):
                sDL.bump(q.dma_start(out=dst[:], in_=src), 16)
            ms['setup_sync'] = sDL.n

        @em.gpsimd
        def _(g):
            for dst, src in ((projz_b, projz_d), (eye_b, eye_d), (ones_b, ones_d),
                             (wcol_b, lnw_d), (bcol_b, lnb_d)):
                sDG.bump(g.dma_start(out=dst[:], in_=src), 16)
            ms['setup_g'] = sDG.n

        @em.vector
        def _(v):
            v.memset(eps5[:], EPS_LN)
            v.memset(eps57[:], EPS_G2)
            mi = v.memset(kv_acc[:], 0.0)
            sV.bump(mi)
            ms['init_v'] = sV.n
        em.flush()

        Iq = ExitStack()

        def sbi(name, shape, dt):
            return Iq.enter_context(nc.sbuf_tensor(name, shape, dt, side="right"))

        wst = [sbi(f"wst{i}", [128, 4096], F32) for i in range(2)]
        cnq = sbi("cnq", [128, 4096], F32)
        gfc = sbi("gfc", [128, 1], F32)
        gpj = sbi("gpj", [128, 1], F32)
        gtmp = sbi("gtmp", [128, 1], F32)
        gsum = sbi("gsum", [1, 1], F32)
        hgf = sbi("hgf", [128, 1], F32)
        hgp = sbi("hgp", [128, 1], F32)
        hgnf = sbi("hgnf", [128, 1], F32)
        hgnp = sbi("hgnp", [128, 1], F32)
        kvev = sbi("kvev", [128, 2048], BF16)
        ps8 = Iq.enter_context(nc.psum_tensor("ps8", [128, 512], BF16))

        gam_tiles = [('fc', i) for i in range(KC)] + [('pj', i) for i in range(JC)]

        @em.vector
        def _(v):
            v.memset(gfc[:], 0.0)
            mi = v.memset(gpj[:], 0.0)
            sV.bump(mi)
            ms['gz0'] = sV.n

        def emit_gamma_tile(i):
            w, idx = gam_tiles[i]

            @em.sync
            def _(q, i=i, w=w, idx=idx):
                if i >= 2:
                    _wk(q, sV, ms, ('gred', i - 2))
                if w == 'fc':
                    src = wfc_d[idx * 128:(idx + 1) * 128, :]
                    dst = wst[i % 2][:, 0:4096]
                else:
                    src = wpj_d[idx * 128:(idx + 1) * 128, :]
                    dst = wst[i % 2][:, 0:1024]
                sWS[i % 2].bump(q.dma_start(out=dst, in_=src), 16)
                ms[('gld', i)] = sWS[i % 2].n

            @em.vector
            def _(v, i=i, w=w, idx=idx):
                if i == 0:
                    _wk(v, sV, ms, 'gz0')
                _wk(v, sWS[i % 2], ms, ('gld', i))
                wid = 4096 if w == 'fc' else 1024
                ri = v.reduce_sum(gtmp[:], wst[i % 2][:, 0:wid], axis=AX.X,
                                  apply_absolute_value=True)
                _w(v, sV, sV.bump(ri))
                acc = gfc if w == 'fc' else gpj
                i2 = v.tensor_tensor(acc[:], acc[:], gtmp[:], ALU.add)
                ms[('gred', i)] = sV.bump(i2)

        A = ExitStack()

        def sba(name, shape, dt):
            return A.enter_context(nc.sbuf_tensor(name, shape, dt))

        wk_b = sba("wk_b", [128, KC * D], BF16)
        wv_b = sba("wv_b", [128, KC * D], BF16)
        xa = [sba(f"xa{i}", [128, KC * LT_A], BF16) for i in range(2)]
        k_sb = sba("k_sb", [128, KC * LT_A], BF16)
        p8k = sba("p8k", [128, LCA * 2048], BF16)
        kp = sba("kp", [128, LCA * H * 256], BF16)
        vtm = sba("vtm", [128, LCA * D], BF16)
        tg1 = [sba(f"tg1{i}", [128, 2048], BF16) for i in range(2)]
        tg2 = [sba(f"tg2{i}", [128, 2048], BF16) for i in range(2)]

        @em.gpsimd
        def _(g):
            for wb, wd in ((wk_b, wk_d), (wv_b, wv_d)):
                src = wd.rearrange("(c p) o -> p c o", p=128)
                dst = wb[:].rearrange("p (c o) -> p c o", c=KC)
                sDG.bump(g.dma_start(out=dst, in_=src), 16)
            ms['wkv'] = sDG.n
            src = x_d[:, 0:LT_A].rearrange("(c p) l -> p c l", p=128)
            dst = xa[0][:].rearrange("p (c l) -> p c l", c=KC)
            sXA[0].bump(g.dma_start(out=dst, in_=src), 16)
            ms[('xa', 0)] = sXA[0].n

        for t in range(NTA):
            if t + 1 < NTA:
                @em.gpsimd
                def _(g, t=t):
                    if t >= 1:
                        _w(g, sT, ms.get(('vmm', t - 1, 2 * LCA - 1)) or ms.get(('kmm', t - 1, KC - 1), 0))
                    src = x_d[:, (t + 1) * LT_A:(t + 2) * LT_A].rearrange("(c p) l -> p c l", p=128)
                    dst = xa[(t + 1) % 2][:].rearrange("p (c l) -> p c l", c=KC)
                    sXA[(t + 1) % 2].bump(g.dma_start(out=dst, in_=src), 16)
                    ms[('xa', t + 1)] = sXA[(t + 1) % 2].n

            xt = xa[t % 2]

            for ocp in range(KC // 2):
                @em.tensor
                def _(T, t=t, xt=xt, ocp=ocp):
                    if ocp == 0:
                        _wk(T, sXA[t % 2], ms, ('xa', t))
                        if t == 0:
                            _wk(T, sDG, ms, 'wkv')
                            _wk(T, sV, ms, 'init_v')
                    for oc in (2 * ocp, 2 * ocp + 1):
                        gi = t * KC + oc
                        if gi >= 4:
                            pt_, po_ = divmod(gi - 4, KC)
                            _wk(T, sS, ms, ('kev', pt_, po_))
                        if t >= 1 and oc < 4:
                            _wk(T, sV, ms, ('vev', t - 1, 4 + oc))
                        pt = ps[oc % 4]
                        for kc in range(KC):
                            mi = T.matmul(pt[:], wk_b[:, kc * D + oc * 128: kc * D + (oc + 1) * 128],
                                          xt[:, kc * LT_A:(kc + 1) * LT_A],
                                          start=(kc == 0), stop=(kc == KC - 1))
                        ms[('kmm', t, oc)] = sT.bump(mi)

                @em.scalar
                def _(S, t=t, ocp=ocp):
                    if t == 0 and ocp == 0:
                        _wk(S, sDL, ms, 'setup_sync')
                    for oc in (2 * ocp, 2 * ocp + 1):
                        _wk(S, sT, ms, ('kmm', t, oc))
                        i = S.activation(k_sb[:, oc * LT_A:(oc + 1) * LT_A], ps[oc % 4][:],
                                         AF.Identity, bias=bkc[:, oc:oc + 1])
                        ms[('kev', t, oc)] = sS.bump(i)

            for lc in (range(LCA) if phases not in ('A1',) else ()):
                for g4p in range(2):
                    @em.tensor
                    def _(T, t=t, lc=lc, g4p=g4p):
                        if lc == 0 and g4p == 0:
                            _wk(T, sS, ms, ('kev', t, KC - 1))
                        for h in range(8 * g4p, 8 * g4p + 8):
                            g4 = h // 4
                            pg = (t * LCA + lc) * 4 + g4
                            if h % 4 == 0:
                                if pg >= 2:
                                    pgt, pgr = divmod(pg - 2, 4 * LCA)
                                    pglc, pgg = divmod(pgr, 4)
                                    _wk(T, sS, ms, ('pev', pgt, pglc, pgg))
                                if t >= 1 and lc == 0 and g4 < 2 and ('kvadd', t - 1, 6 + g4) in ms:
                                    _wk(T, sV, ms, ('kvadd', t - 1, 6 + g4))
                            bank = ps[4 + (g4 % 2)]
                            half = h % 2
                            oc = h // 2
                            lhsT = k_sb[:, oc * LT_A + lc * 128: oc * LT_A + (lc + 1) * 128]
                            rhs = projz_b[:, half * 128:(half + 1) * 128]
                            mi = T.matmul(bank[:, (h % 4) * 128:(h % 4 + 1) * 128], lhsT, rhs,
                                          start=True, stop=True, skip_group_check=True)
                            if h % 4 == 3:
                                ms[('pmm', t, lc, g4)] = sT.bump(mi)

                    @em.scalar
                    def _(S, t=t, lc=lc, g4p=g4p):
                        for g4 in (2 * g4p, 2 * g4p + 1):
                            _wk(S, sT, ms, ('pmm', t, lc, g4))
                            if g4 == 0 and t >= 1:
                                _wk(S, sV, ms, ('trg1', t - 1, lc))
                            i = S.activation(p8k[:, lc * 2048 + g4 * 512: lc * 2048 + (g4 + 1) * 512],
                                             ps[4 + (g4 % 2)][:], AF.Identity)
                            ms[('pev', t, lc, g4)] = sS.bump(i)

                if phases == 'A2a':
                    ms[('trg1', t, lc)] = None
                    ms[('sh', t, lc)] = None
                    ms[('cos', t, lc)] = None
                    continue

                @em.vector
                def _(v, t=t, lc=lc):
                    _wk(v, sS, ms, ('pev', t, lc, 3))
                    p8 = p8k[:, lc * 2048:(lc + 1) * 2048]
                    g1, g2 = tg1[lc % 2], tg2[lc % 2]
                    v.tensor_single_scalar(g1[:], p8, -B8, ALU.is_lt)
                    v.scalar_tensor_tensor(g2[:], p8, B8, g1[:], ALU.is_gt, ALU.subtract)
                    v.scalar_tensor_tensor(g1[:], g2[:], -C1X8, p8, ALU.mult, ALU.add)
                    v.scalar_tensor_tensor(g2[:], g2[:], -C2X8, g1[:], ALU.mult, ALU.add)
                    i = v.tensor_scalar(g1[:], g2[:], -B8, B8, ALU.max, ALU.min)
                    ms[('trg1', t, lc)] = sV.bump(i)

                for lcd in ([lc - 1] if lc >= 1 else []) + ([LCA - 1] if lc == LCA - 1 else []):
                    @em.scalar
                    def _(S, t=t, lcd=lcd):
                        _wk(S, sV, ms, ('trg1', t, lcd))
                        if lcd == 0 and t >= 1 and ('kvmm', t - 1, KC - 1) in ms:
                            _wk(S, sT, ms, ('kvmm', t - 1, KC - 1))
                        base = lcd * H * 256
                        W = LCA * H * 256
                        sin_out = bass.AP(kp, base + 128, [[W, 128], [256, H], [1, 128]])
                        S.activation(sin_out, tg1[lcd % 2][:], AF.Sin, scale=0.125)
                        i = S.activation(tg2[lcd % 2][:], tg1[lcd % 2][:], AF.Sin, scale=0.0625)
                        ms[('sh', t, lcd)] = sS.bump(i)

                    @em.vector
                    def _(v, t=t, lcd=lcd):
                        _wk(v, sS, ms, ('sh', t, lcd))
                        g2 = tg2[lcd % 2]
                        v.tensor_tensor(g2[:], g2[:], g2[:], ALU.mult)
                        base = lcd * H * 256
                        W = LCA * H * 256
                        cos_out = bass.AP(kp, base, [[W, 128], [256, H], [1, 128]])
                        i = v.tensor_scalar(cos_out, g2[:], -2.0, 1.0, ALU.mult, ALU.add)
                        ms[('cos', t, lcd)] = sV.bump(i)

            for lc in (range(LCA) if phases not in ('A1', 'A2', 'A2a') else ()):
                @em.tensor
                def _(T, t=t, xt=xt, lc=lc):
                    for half in range(2):
                        idx = lc * 2 + half
                        gi = t * 2 * LCA + idx
                        if gi >= 4:
                            vt_, vi_ = divmod(gi - 4, 2 * LCA)
                            _wk(T, sV, ms, ('vev', vt_, vi_))
                        if idx < 4:
                            _wk(T, sS, ms, ('kev', t, 4 + idx))
                        pt = ps[idx % 4]
                        for kc in range(KC):
                            mi = T.matmul(pt[:],
                                          xt[:, kc * LT_A + lc * 128: kc * LT_A + (lc + 1) * 128],
                                          wv_b[:, kc * D + half * 512: kc * D + (half + 1) * 512],
                                          start=(kc == 0), stop=(kc == KC - 1))
                        ms[('vmm', t, idx)] = sT.bump(mi)

                @em.vector
                def _(v, t=t, lc=lc):
                    if t == 0 and lc == 0:
                        _wk(v, sDL, ms, 'setup_sync')
                    for half in range(2):
                        idx = lc * 2 + half
                        _wk(v, sT, ms, ('vmm', t, idx))
                        i = v.scalar_tensor_tensor(
                            vtm[:, lc * D + half * 512: lc * D + (half + 1) * 512],
                            ps[idx % 4][:], 0.0, bvb[:, half * 512:(half + 1) * 512],
                            ALU.add, ALU.add)
                        ms[('vev', t, idx)] = sV.bump(i)

            for pair in (range(KC) if phases not in ('A1', 'A2', 'A2a', 'A3') else ()):
                @em.tensor
                def _(T, t=t, pair=pair):
                    if pair == 0:
                        _wk(T, sV, ms, ('cos', t, LCA - 1))
                        _wk(T, sV, ms, ('vev', t, 2 * LCA - 1))
                    gi = t * KC + pair
                    if gi >= 2:
                        kt_, kp_ = divmod(gi - 2, KC)
                        _wk(T, sV, ms, ('kvadd', kt_, kp_))
                    pt = ps[4 + pair % 2]
                    for lc in range(LCA):
                        st_, sp_ = (lc == 0), (lc == LCA - 1)
                        T.matmul(pt[0:64, 0:256],
                                 vtm[:, lc * D + pair * 128: lc * D + pair * 128 + 64],
                                 kp[:, lc * H * 256 + (2 * pair) * 256: lc * H * 256 + (2 * pair + 1) * 256],
                                 start=st_, stop=sp_, tile_position=(0, 0), skip_group_check=True)
                        mi = T.matmul(pt[64:128, 0:256],
                                      vtm[:, lc * D + pair * 128 + 64: lc * D + (pair + 1) * 128],
                                      kp[:, lc * H * 256 + (2 * pair + 1) * 256: lc * H * 256 + (2 * pair + 2) * 256],
                                      start=st_, stop=sp_, tile_position=(0, 64), skip_group_check=True)
                    ms[('kvmm', t, pair)] = sT.bump(mi)

                @em.vector
                def _(v, t=t, pair=pair):
                    _wk(v, sT, ms, ('kvmm', t, pair))
                    sl = kv_acc[:, pair * 256:(pair + 1) * 256]
                    i = v.tensor_tensor(sl, sl, ps[4 + pair % 2][:, 0:256], ALU.add)
                    ms[('kvadd', t, pair)] = sV.bump(i)

            if phases == 'full':
                for gi in range(5 * t, 5 * t + 5):
                    emit_gamma_tile(gi)
            em.flush()

        ms['A_T_end'] = sT.n
        ms['A_V_end'] = ms.get(('kvadd', NTA - 1, KC - 1), sV.n)
        ms['A_S_end'] = sS.n

        if phases in ('A', 'A1', 'A2', 'A2a', 'A3'):
            @em.sync
            def _(q):
                if phases == 'A':
                    _wk(q, sV, ms, 'A_V_end')
                    sO.bump(q.dma_start(out=out_d[0:128, 0:2048], in_=kv_acc[:]), 16)
                q.nop()

            @em.gpsimd
            def _(g):
                if phases == 'A1':
                    _wk(g, sS, ms, ('kev', NTA - 1, KC - 1))
                    sO.bump(g.dma_start(out=out_d[0:128, 0:KC * LT_A], in_=k_sb[:]), 16)
                elif phases == 'A2':
                    _wk(g, sV, ms, ('cos', NTA - 1, LCA - 1))
                    sO.bump(g.dma_start(out=out_d[0:128, 0:4096], in_=kp[:, 0:4096]), 16)
                elif phases == 'A2a':
                    _wk(g, sS, ms, ('pev', NTA - 1, LCA - 1, 3))
                    sO.bump(g.dma_start(out=out_d[0:128, 0:4096], in_=p8k[:, 0:4096]), 16)
                elif phases == 'A3':
                    _wk(g, sV, ms, ('vev', NTA - 1, 2 * LCA - 1))
                    sO.bump(g.dma_start(out=out_d[0:128, 0:LCA * D], in_=vtm[:]), 16)
                g.wait_ge(sO.h, sO.n)
            em.flush()
            A.close()
            perm.pop_all()
            return nc
        A.close()

        I = ExitStack()
        wq_b = I.enter_context(nc.sbuf_tensor("wq_b", [128, KC * D], BF16))
        wo_b = I.enter_context(nc.sbuf_tensor("wo_b", [128, KC * D], BF16))
        wfc8 = I.enter_context(nc.sbuf_tensor("wfc8", [128, KC * 4096], FP8))
        wpj8 = I.enter_context(nc.sbuf_tensor("wpj8", [128, JC * D], FP8))
        kvT = I.enter_context(nc.sbuf_tensor("kvT", [128, 2048], BF16))
        ucol = I.enter_context(nc.sbuf_tensor("ucol", [128, 8], F32))
        ncol = I.enter_context(nc.sbuf_tensor("ncol", [128, 8], F32))

        if phases == 'AI1':
            @em.gpsimd
            def _(g):
                _wk(g, sV, ms, ('gred', len(gam_tiles) - 1))
                sO.bump(g.dma_start(out=out_d[0:128, 0:1], in_=gfc[:], allow_slow_non_contiguous=True), 16)
                sO.bump(g.dma_start(out=out_d[0:128, 1:2], in_=gpj[:], allow_slow_non_contiguous=True), 16)
                g.wait_ge(sO.h, sO.n)
            em.flush()
            Iq.close()
            return nc

        for wname, acc, hg, hgn in (('fc', gfc, hgf, hgnf), ('pj', gpj, hgp, hgnp)):
            @em.tensor
            def _(T, acc=acc, wname=wname):
                _wk(T, sV, ms, ('gred', len(gam_tiles) - 1))
                _wk(T, sDL, ms, 'setup_sync')
                if wname == 'pj':
                    _wk(T, sS, ms, ('gfin_s', 'fc'))
                mi = T.matmul(psS[0:1, 0:1], acc[:], ones_f[:, 0:1], start=True, stop=True)
                ms[('gfin_t1', wname)] = sT.bump(mi)

            @em.scalar
            def _(S, wname=wname):
                _wk(S, sT, ms, ('gfin_t1', wname))
                i = S.activation(gsum[:], psS[0:1, 0:1], AF.Identity)
                ms[('gfin_s1', wname)] = sS.bump(i)

            @em.tensor
            def _(T, wname=wname):
                _wk(T, sS, ms, ('gfin_s1', wname))
                mi = T.matmul(psS[:, 0:1], ones_f[0:1, :], gsum[:], start=True, stop=True)
                ms[('gfin_t2', wname)] = sT.bump(mi)

            @em.scalar
            def _(S, hg=hg, wname=wname):
                _wk(S, sT, ms, ('gfin_t2', wname))
                i = S.activation(hg[:], psS[:, 0:1], AF.Identity, scale=INV2_23,
                                 bias=eps57[:, 0:1])
                ms[('gfin_s', wname)] = sS.bump(i)

            @em.vector
            def _(v, hg=hg, hgn=hgn, wname=wname):
                _wk(v, sS, ms, ('gfin_s', wname))
                i = v.tensor_single_scalar(hgn[:], hg[:], -1.0, ALU.mult)
                ms[('gfin_v', wname)] = sV.bump(i)

        for i, (w, idx) in enumerate(gam_tiles):
            @em.sync
            def _(q, i=i, w=w, idx=idx):
                if i == 0:
                    _wk(q, sV, ms, ('gfin_v', 'pj'))
                if i >= 2:
                    _wk(q, sV, ms, ('qdone', i - 2))
                if w == 'fc':
                    src = wfc_d[idx * 128:(idx + 1) * 128, :]
                    dst = wst[i % 2][:, 0:4096]
                else:
                    src = wpj_d[idx * 128:(idx + 1) * 128, :]
                    dst = wst[i % 2][:, 0:1024]
                sWS[i % 2].bump(q.dma_start(out=dst, in_=src), 16)
                ms[('qld', i)] = sWS[i % 2].n

            @em.vector
            def _(v, i=i, w=w, idx=idx):
                _wk(v, sWS[i % 2], ms, ('qld', i))
                wid = 4096 if w == 'fc' else 1024
                hg = hgf if w == 'fc' else hgp
                hgn = hgnf if w == 'fc' else hgnp
                wt = wst[i % 2][:, 0:wid]
                v.tensor_single_scalar(cnq[:, 0:wid], wt, hgn[:, 0:1], ALU.is_lt)
                if w == 'fc':
                    out = wfc8[:, idx * 4096:(idx + 1) * 4096]
                else:
                    out = wpj8[:, idx * D:(idx + 1) * D]
                i2 = v.scalar_tensor_tensor(out, wt, hg[:, 0:1], cnq[:, 0:wid],
                                            ALU.is_gt, ALU.subtract)
                ms[('qdone', i)] = sV.bump(i2)


        for which, rhs_cols, outcol, scl in (('u', wcol_b, ucol, 1.0), ('c', bcol_b, ncol, -1.0)):
            for oc in range(KC):
                @em.tensor
                def _(T, rhs_cols=rhs_cols, oc=oc, which=which):
                    if which == 'u' and oc == 0:
                        _wk(T, sV, ms, ('qdone', len(gam_tiles) - 1))
                        _wk(T, sS, ms, ('gfin_s', 'pj'))
                        _wk(T, sDG, ms, 'setup_g')
                    else:
                        prev = ('c', oc - 1) if which == 'c' and oc > 0 else \
                               (('u', KC - 1) if which == 'c' else ('u', oc - 1))
                        _wk(T, sS, ms, ('ucev',) + prev)
                    for jc in range(JC):
                        mi = T.matmul(psS[:, 0:1],
                                      wpj8[:, jc * D + oc * 128: jc * D + (oc + 1) * 128],
                                      rhs_cols[:, jc:jc + 1],
                                      start=(jc == 0), stop=(jc == JC - 1))
                    ms[('ucmm', which, oc)] = sT.bump(mi)

                @em.scalar
                def _(S, outcol=outcol, oc=oc, which=which, scl=scl):
                    _wk(S, sT, ms, ('ucmm', which, oc))
                    i = S.activation(outcol[:, oc:oc + 1], psS[:, 0:1], AF.Identity, scale=scl)
                    ms[('ucev', which, oc)] = sS.bump(i)

        @em.scalar
        def _(S):
            _wk(S, sV, ms, 'A_V_end')
            i = S.activation(kvev[:], kv_acc[:], AF.Identity, scale=float(2.0 ** -7))
            ms['kvev'] = sS.bump(i)

        for pair in range(KC):
            for mc in range(2):
                idx = pair * 2 + mc

                @em.tensor
                def _(T, pair=pair, mc=mc, idx=idx):
                    if idx == 0:
                        _wk(T, sDG, ms, 'setup_g')
                    _w(T, sS, (ms.get('kvev', 0) if idx == 0 else ms.get(('kvtev', idx - 1), 0)))
                    src = kvev[:, pair * 256 + mc * 128: pair * 256 + (mc + 1) * 128]
                    mi = T.transpose(ps8[:, 0:128], src, eye_b[:])
                    ms[('kvtmm', idx)] = sT.bump(mi)

                @em.scalar
                def _(S, pair=pair, mc=mc, idx=idx):
                    _wk(S, sT, ms, ('kvtmm', idx))
                    he = 2 * pair
                    ho = 2 * pair + 1
                    S.activation(kvT[:, (he * 2 + mc) * 64:(he * 2 + mc + 1) * 64],
                                 ps8[:, 0:64], AF.Identity)
                    i = S.activation(kvT[:, (ho * 2 + mc) * 64:(ho * 2 + mc + 1) * 64],
                                     ps8[:, 64:128], AF.Identity)
                    ms[('kvtev', idx)] = sS.bump(i)


        @em.gpsimd
        def _(g):
            _wk(g, sT, ms, 'A_T_end')
            _wk(g, sS, ms, 'A_S_end')
            _wk(g, sV, ms, 'A_V_end')
            for wb, wd in ((wq_b, wq_d), (wo_b, wo_d)):
                src = wd.rearrange("(c p) o -> p c o", p=128)
                dst = wb[:].rearrange("p (c o) -> p c o", c=KC)
                sDG.bump(g.dma_start(out=dst, in_=src), 16)
            ms['wqwo'] = sDG.n

        if phases == 'AI':
            @em.gpsimd
            def _(g):
                _wk(g, sS, ms, ('kvtev', 2 * KC - 1))
                _wk(g, sV, ms, ('qdone', len(gam_tiles) - 1))
                _wk(g, sS, ms, ('ucev', 'c', KC - 1))
                sO.bump(g.dma_start(out=out_d[0:128, 0:2048], in_=kvT[:]), 16)
                sO.bump(g.dma_start(out=out_d[0:128, 2048:2056], in_=ucol[:]), 16)
                sO.bump(g.dma_start(out=out_d[0:128, 2056:2064], in_=ncol[:]), 16)
                sO.bump(g.dma_start(out=out_d[0:128, 2064:2065], in_=hgf[:], allow_slow_non_contiguous=True), 16)
                sO.bump(g.dma_start(out=out_d[0:128, 2065:2066], in_=hgp[:], allow_slow_non_contiguous=True), 16)
                sO.bump(g.dma_start(out=out_d[0:128, 2066:2067], in_=gfc[:], allow_slow_non_contiguous=True), 16)
                sO.bump(g.dma_start(out=out_d[128:256, 0:2048], in_=kvev[:]), 16)
                sO.bump(g.dma_start(out=out_d[256:384, 0:4096], in_=wpj8[:, 0:4096]), 16)
                g.wait_ge(sO.h, sO.n)
            em.flush()
            Iq.close()
            I.pop_all()
            perm.pop_all()
            return nc
        em.flush()
        Iq.close()

        Bk = ExitStack()

        def sbb(name, shape, dt):
            return Bk.enter_context(nc.sbuf_tensor(name, shape, dt))

        xf = sbb("xf", [128, KC * LT_B], F32)
        xb = sbb("xb", [128, KC * LT_B], BF16)
        u1q = sbb("u1q", [128, KC * LT_B], BF16)
        u1o = sbb("u1o", [128, KC * LT_B], BF16)
        p8q = sbb("p8q", [128, H * LT_B], BF16)
        tq1 = sbb("tq1", [128, H * LT_B // 2], BF16)
        tq2 = sbb("tq2", [128, H * LT_B // 2], BF16)
        qps = sbb("qps", [128, H * LT_B], BF16)
        qpc = p8q
        zt = [sbb(f"zt{i}", [128, KC * LT_B], F32) for i in range(2)]
        x1b = [sbb(f"x1b{i}", [128, KC * LT_B], BF16) for i in range(2)]
        gb = sbb("gb", [128, JC * LT_B], BF16)
        rsbF = sbb("rsbF", [128, LT_B], F32)
        gq = sbb("gq", [128, 4 * LT_B], BF16)
        s1f = sbb("s1f", [128, LT_B], F32)
        s2b = sbb("s2b", [128, LT_B], BF16)
        psS2 = Bk.enter_context(nc.psum_tensor("psS2", [128, 512], F32))
        m1 = sbb("m1", [128, LT_B], F32)
        m2 = sbb("m2", [128, LT_B], F32)
        rsb = sbb("rsb", [128, LT_B], F32)
        mrs = sbb("mrs", [128, LT_B], F32)
        t2b = sbb("t2b", [128, LT_B], F32)
        t1b = m1

        @em.sync
        def _(q):
            _wk(q, sV, ms, ('qdone', len(gam_tiles) - 1))
            _wk(q, sS, ms, ('kvtev', 2 * KC - 1))
            src = x_d[:, 0:LT_B].rearrange("(c p) l -> p c l", p=128)
            dst = xf[:].rearrange("p (c l) -> p c l", c=KC)
            sXF[0].bump(q.dma_start(out=dst, in_=src), 16)
            ms[('xf', 0)] = sXF[0].n

        @em.gpsimd
        def _(g):
            _wk(g, sV, ms, ('qdone', len(gam_tiles) - 1))
            _wk(g, sS, ms, ('kvtev', 2 * KC - 1))
            src = x_d[:, 0:LT_B].rearrange("(c p) l -> p c l", p=128)
            dst = xb[:].rearrange("p (c l) -> p c l", c=KC)
            sXB[0].bump(g.dma_start(out=dst, in_=src), 16)
            ms[('xbB', 0)] = sXB[0].n

        def emit_xb_load(t):
            @em.gpsimd
            def _(g, t=t):
                _wk(g, sT, ms, ('qmm', t - 1, KC - 1))
                src = x_d[:, t * LT_B:(t + 1) * LT_B].rearrange("(c p) l -> p c l", p=128)
                dst = xb[:].rearrange("p (c l) -> p c l", c=KC)
                sXB[t % 2].bump(g.dma_start(out=dst, in_=src), 16)
                ms[('xbB', t)] = sXB[t % 2].n

        def emit_ln_tree(tag, t, src, sqdst, pre_key):
            @em.vector
            def _(v, t=t):
                _wk(v, sV, ms, pre_key)
                for oc in range(KC):
                    sl = src[:, oc * LT_B:(oc + 1) * LT_B]
                    v.tensor_tensor(sqdst[:, oc * LT_B:(oc + 1) * LT_B], sl, sl, ALU.mult)
                v.tensor_tensor(s1f[:], src[:, 0:LT_B], src[:, LT_B:2 * LT_B], ALU.add)
                for oc in range(2, KC):
                    v.tensor_tensor(s1f[:], s1f[:], src[:, oc * LT_B:(oc + 1) * LT_B], ALU.add)
                v.tensor_tensor(s2b[:], sqdst[:, 0:LT_B], sqdst[:, LT_B:2 * LT_B], ALU.add)
                for oc in range(2, KC):
                    mi = v.tensor_tensor(s2b[:], s2b[:], sqdst[:, oc * LT_B:(oc + 1) * LT_B], ALU.add)
                ms[(f'{tag}tree', t)] = sV.bump(mi)

        def emit_ln_smm(tag, t, gate_keys=()):
            @em.tensor
            def _(T, t=t):
                _wk(T, sV, ms, (f'{tag}tree', t))
                for gk in gate_keys:
                    _wk(T, sS, ms, gk)
                if tag == 'l1':
                    if t == 0:
                        _wk(T, sDL, ms, 'setup_sync')
                        _wk(T, sDG, ms, 'setup_g')
                    d1, d2 = psS[:, 0:LT_B], psS[:, LT_B:2 * LT_B]
                else:
                    if t >= 1:
                        _wk(T, sV, ms, ('l2var', t - 1))
                    d1, d2 = ps[0][:, 0:LT_B], ps[1][:, 0:LT_B]
                T.matmul(d1, ones_f[:], s1f[:], start=True, stop=True,
                         skip_group_check=True)
                mi = T.matmul(d2, ones_b[:], s2b[:], start=True, stop=True,
                              skip_group_check=True)
                ms[(f'{tag}smm', t)] = sT.bump(mi)

        def emit_ln_var(tag, t):
            @em.vector
            def _(v, t=t):
                _wk(v, sT, ms, (f'{tag}smm', t))
                if tag == 'l1':
                    d1, d2 = psS[:, 0:LT_B], psS[:, LT_B:2 * LT_B]
                else:
                    d1, d2 = ps[0][:, 0:LT_B], ps[1][:, 0:LT_B]
                v.tensor_single_scalar(m1[:], d1, float(2.0 ** -10), ALU.mult)
                v.tensor_tensor(m2[:], m1[:], m1[:], ALU.mult)
                i = v.scalar_tensor_tensor(m2[:], d2, float(2.0 ** -10),
                                           m2[:], ALU.mult, ALU.subtract)
                ms[(f'{tag}var', t)] = sV.bump(i)

        def emit_ln_sd(tag, t):
            @em.scalar
            def _(S, t=t):
                _wk(S, sV, ms, (f'{tag}var', t))
                i = S.activation(m2[:], m2[:], AF.Sqrt, bias=eps5[:, 0:1])
                ms[(f'{tag}sd', t)] = sS.bump(i)

        def emit_ln_fin(tag, t, src):
            @em.vector
            def _(v, t=t):
                _wk(v, sS, ms, (f'{tag}sd', t))
                ri = v.reciprocal(rsb[:], m2[:])
                _w(v, sV, sV.bump(ri))
                if tag == 'l1' and t >= 2:
                    _wk(v, sT, ms, ('hmm', t - 2, JC - 1))
                for oc in range(KC):
                    sl = src[:, oc * LT_B:(oc + 1) * LT_B]
                    v.tensor_tensor(sl, sl, m1[:], ALU.subtract)
                    if tag == 'l1':
                        mi = v.tensor_tensor(x1b[t % 2][:, oc * LT_B:(oc + 1) * LT_B],
                                             sl, rsb[:], ALU.mult)
                    else:
                        mi = v.tensor_tensor(sl, sl, rsb[:], ALU.mult)
                ms[(f'{tag}fin', t)] = sV.bump(mi)

        def emit_out_dma(t):
            @em.sync
            def _(q, t=t):
                _wk(q, sV, ms, ('l2fin', t))
                src = zt[t % 2][:].rearrange("p (c l) -> p c l", c=KC)
                dst = out_d[:, t * LT_B:(t + 1) * LT_B].rearrange("(c p) l -> p c l", p=128)
                sO.bump(q.dma_start(out=dst, in_=src), 16)
                ms[('outdma', t)] = sO.n

        HW2 = H * LT_B // 2

        def emit_frontend(t):
            for ocp in range(KC // 2):
                @em.tensor
                def _(T, t=t, ocp=ocp):
                    if ocp == 0:
                        _wk(T, sXB[t % 2], ms, ('xbB', t))
                        if t == 0:
                            _wk(T, sDG, ms, 'wqwo')
                            _wk(T, sV, ms, 'A_V_end')
                    for oc in (2 * ocp, 2 * ocp + 1):
                        if oc >= 4:
                            _wk(T, sS, ms, ('qev', t, ((oc - 4) // 2) * 2 + 1))
                        elif t >= 1:
                            _wk(T, sS, ms, ('yev', t - 1, 4 + 2 * (oc // 2) + 1))
                        pt = ps[(oc // 2) % 2]
                        col = (oc % 2) * LT_B
                        for kc in range(KC):
                            mi = T.matmul(pt[:, col:col + LT_B],
                                          wq_b[:, kc * D + oc * 128: kc * D + (oc + 1) * 128],
                                          xb[:, kc * LT_B:(kc + 1) * LT_B],
                                          start=(kc == 0), stop=(kc == KC - 1))
                        ms[('qmm', t, oc)] = sT.bump(mi)

                @em.scalar
                def _(S, t=t, ocp=ocp):
                    _wk(S, sT, ms, ('qmm', t, 2 * ocp + 1))
                    if ocp == 0 and t == 0:
                        _wk(S, sDL, ms, 'setup_sync')
                    for oc in (2 * ocp, 2 * ocp + 1):
                        col = (oc % 2) * LT_B
                        i = S.activation(u1q[:, oc * LT_B:(oc + 1) * LT_B],
                                         ps[(oc // 2) % 2][:, col:col + LT_B],
                                         AF.Identity, bias=bqc[:, oc:oc + 1])
                        ms[('qev', t, oc)] = sS.bump(i)

            for gp in range(KC // 2):
                @em.tensor
                def _(T, t=t, gp=gp):
                    for grp in (2 * gp, 2 * gp + 1):
                        if grp == 0:
                            _wk(T, sS, ms, ('qev', t, 5))
                        elif grp == 1:
                            _wk(T, sS, ms, ('qev', t, 7))
                        else:
                            _wk(T, sS, ms, ('pqev', t, grp - 2))
                        bank = ps[grp % 2]
                        for h in (2 * grp, 2 * grp + 1):
                            half = h % 2
                            oc = h // 2
                            mi = T.matmul(bank[:, half * LT_B:(half + 1) * LT_B],
                                          projz_b[:, half * 128:(half + 1) * 128],
                                          u1q[:, oc * LT_B:(oc + 1) * LT_B],
                                          start=True, stop=True, skip_group_check=True)
                        ms[('pqmm', t, grp)] = sT.bump(mi)

                @em.scalar
                def _(S, t=t, gp=gp):
                    for grp in (2 * gp, 2 * gp + 1):
                        _wk(S, sT, ms, ('pqmm', t, grp))
                        if grp == 0 and t >= 1:
                            _wk(S, sT, ms, ('omm', t - 1, KC - 1))
                        i = S.activation(p8q[:, grp * 512:(grp + 1) * 512],
                                         ps[grp % 2][:], AF.Identity)
                        ms[('pqev', t, grp)] = sS.bump(i)

        def emit_trig_rr(t, hf):
            @em.vector
            def _(v, t=t, hf=hf):
                _wk(v, sS, ms, ('pqev', t, 4 * hf + 3))
                p8 = p8q[:, hf * HW2:(hf + 1) * HW2]
                v.tensor_single_scalar(tq1[:], p8, -B8, ALU.is_lt)
                v.scalar_tensor_tensor(tq2[:], p8, B8, tq1[:], ALU.is_gt, ALU.subtract)
                v.scalar_tensor_tensor(tq1[:], tq2[:], -C1X8, p8, ALU.mult, ALU.add)
                v.scalar_tensor_tensor(tq2[:], tq2[:], -C2X8, tq1[:], ALU.mult, ALU.add)
                i = v.tensor_scalar(tq1[:], tq2[:], -B8, B8, ALU.max, ALU.min)
                ms[('trq1h', t, hf)] = sV.bump(i)

        def emit_trig_sin(t, hf):
            @em.scalar
            def _(S, t=t, hf=hf):
                _wk(S, sV, ms, ('trq1h', t, hf))
                S.activation(qps[:, hf * HW2:(hf + 1) * HW2], tq1[:], AF.Sin, scale=0.125)
                i = S.activation(tq2[:], tq1[:], AF.Sin, scale=0.0625)
                ms[('shqh', t, hf)] = sS.bump(i)

        def emit_trig_cos(t, hf):
            @em.vector
            def _(v, t=t, hf=hf):
                _wk(v, sS, ms, ('shqh', t, hf))
                v.tensor_tensor(tq2[:], tq2[:], tq2[:], ALU.mult)
                i = v.tensor_scalar(qpc[:, hf * HW2:(hf + 1) * HW2], tq2[:],
                                    -2.0, 1.0, ALU.mult, ALU.add)
                ms[('cosqh', t, hf)] = sV.bump(i)

        def emit_gw_corr(tp, has_next_o):
            for ocp in range(KC // 2):
                @em.tensor
                def _(T, tp=tp, ocp=ocp):
                    if ocp == 0:
                        _wk(T, sV, ms, ('s2a', tp, JC - 1))
                    for oc in (2 * ocp, 2 * ocp + 1):
                        if oc >= 4:
                            _wk(T, sV, ms, ('corr', tp, ((oc - 4) // 2) * 2 + 1))
                        elif has_next_o:
                            _wk(T, sS, ms, ('oevp', tp + 1, 2 + oc // 2))
                        pt = ps[2 + (oc // 2) % 2]
                        col = (oc % 2) * LT_B
                        for jc in range(JC):
                            mi = T.matmul(pt[:, col:col + LT_B],
                                          wpj8[:, jc * D + oc * 128: jc * D + (oc + 1) * 128],
                                          gb[:, jc * LT_B:(jc + 1) * LT_B],
                                          start=(jc == 0), stop=(jc == JC - 1))
                        ms[('gwmm', tp, oc)] = sT.bump(mi)

                @em.vector
                def _(v, tp=tp, ocp=ocp):
                    if ocp == 0:
                        _wk(v, sV, ms, ('ffin', tp))
                    _wk(v, sT, ms, ('gwmm', tp, 2 * ocp + 1))
                    for oc in (2 * ocp, 2 * ocp + 1):
                        col = (oc % 2) * LT_B
                        v.tensor_scalar(t1b[:], mrs[:], ucol[:, oc:oc + 1], ncol[:, oc:oc + 1],
                                        ALU.mult, ALU.add)
                        v.tensor_tensor(t2b[:], ps[2 + (oc // 2) % 2][:, col:col + LT_B],
                                        rsbF[:], ALU.mult)
                        v.scalar_tensor_tensor(t2b[:], t1b[:], -1.0, t2b[:], ALU.mult, ALU.add)
                        sl = zt[tp % 2][:, oc * LT_B:(oc + 1) * LT_B]
                        i = v.tensor_tensor(sl, t2b[:],
                                            x1b[tp % 2][:, oc * LT_B:(oc + 1) * LT_B], ALU.add)
                        ms[('corr', tp, oc)] = sV.bump(i)

        emit_frontend(0)
        for hf in range(2):
            emit_trig_rr(0, hf)
            emit_trig_sin(0, hf)
            emit_trig_cos(0, hf)
        emit_xb_load(1)

        for t in range(NTB):
            ztc = zt[t % 2]
            ztp = zt[(t - 1) % 2]

            if t + 1 < NTB:
                @em.sync
                def _(q, t=t):
                    _wk(q, sV, ms, ('zadd', t))
                    src = x_d[:, (t + 1) * LT_B:(t + 2) * LT_B].rearrange("(c p) l -> p c l", p=128)
                    dst = xf[:].rearrange("p (c l) -> p c l", c=KC)
                    sXF[(t + 1) % 2].bump(q.dma_start(out=dst, in_=src), 16)
                    ms[('xf', t + 1)] = sXF[(t + 1) % 2].n

            for pp in range(KC // 2):
                @em.tensor
                def _(T, t=t, pp=pp):
                    if pp == 0:
                        _wk(T, sV, ms, ('cosqh', t, 1))
                        if t == 0:
                            _wk(T, sS, ms, ('kvtev', 2 * KC - 1))
                    for pair in (2 * pp, 2 * pp + 1):
                        if pair % 2 == 0:
                            if pair >= 4:
                                _wk(T, sS, ms, ('oevp', t, pp - 2))
                            elif t >= 2:
                                _wk(T, sV, ms, ('corr', t - 2, 5 + pair))
                        bank = ps[2 + (pair // 2) % 2]
                        col = (pair % 2) * LT_B
                        for sub in range(2):
                            h = 2 * pair + sub
                            tp = (0, 0) if sub == 0 else (0, 64)
                            T.matmul(bank[sub * 64:(sub + 1) * 64, col:col + LT_B],
                                     kvT[:, (h * 2) * 64:(h * 2 + 1) * 64],
                                     qpc[:, h * LT_B:(h + 1) * LT_B],
                                     start=True, stop=False, tile_position=tp, skip_group_check=True)
                            mi = T.matmul(bank[sub * 64:(sub + 1) * 64, col:col + LT_B],
                                          kvT[:, (h * 2 + 1) * 64:(h * 2 + 2) * 64],
                                          qps[:, h * LT_B:(h + 1) * LT_B],
                                          start=False, stop=True, tile_position=tp, skip_group_check=True)
                    ms[('omm', t, 2 * pp + 1)] = sT.bump(mi)

                @em.scalar
                def _(S, t=t, pp=pp):
                    _wk(S, sT, ms, ('omm', t, 2 * pp + 1))
                    i = S.activation(u1o[:, 2 * pp * LT_B:(2 * pp + 2) * LT_B],
                                     ps[2 + pp % 2][:, 0:2 * LT_B], AF.Identity)
                    ms[('oevp', t, pp)] = sS.bump(i)

            for ocp in range(KC // 2):
                @em.tensor
                def _(T, t=t, ocp=ocp):
                    if ocp == 0:
                        _wk(T, sS, ms, ('oevp', t, KC // 2 - 1))
                        if t >= 2:
                            _wk(T, sV, ms, ('l2var', t - 2))
                    for oc in (2 * ocp, 2 * ocp + 1):
                        if oc >= 4:
                            _wk(T, sS, ms, ('yev', t, ((oc - 4) // 2) * 2 + 1))
                        elif oc < 4:
                            _wk(T, sS, ms, ('pqev', t, 6 + (oc // 2) % 2))
                        pt = ps[(oc // 2) % 2]
                        col = (oc % 2) * LT_B
                        for kc in range(KC):
                            mi = T.matmul(pt[:, col:col + LT_B],
                                          wo_b[:, kc * D + oc * 128: kc * D + (oc + 1) * 128],
                                          u1o[:, kc * LT_B:(kc + 1) * LT_B],
                                          start=(kc == 0), stop=(kc == KC - 1))
                        ms[('ymm', t, oc)] = sT.bump(mi)

                @em.scalar
                def _(S, t=t, ocp=ocp, ztc=ztc):
                    _wk(S, sT, ms, ('ymm', t, 2 * ocp + 1))
                    if ocp == 0 and t >= 2:
                        _wk(S, sO, ms, ('outdma', t - 2))
                    for oc in (2 * ocp, 2 * ocp + 1):
                        col = (oc % 2) * LT_B
                        i = S.activation(ztc[:, oc * LT_B:(oc + 1) * LT_B],
                                         ps[(oc // 2) % 2][:, col:col + LT_B],
                                         AF.Identity, bias=boc[:, oc:oc + 1])
                        ms[('yev', t, oc)] = sS.bump(i)

            if t + 1 < NTB:
                emit_frontend(t + 1)
                if t + 2 < NTB:
                    emit_xb_load(t + 2)

            @em.vector
            def _(v, t=t, ztc=ztc):
                _wk(v, sS, ms, ('yev', t, KC - 1))
                _wk(v, sXF[t % 2], ms, ('xf', t))
                for oc in range(KC):
                    sl = ztc[:, oc * LT_B:(oc + 1) * LT_B]
                    mi = v.tensor_tensor(sl, sl, xf[:, oc * LT_B:(oc + 1) * LT_B], ALU.add)
                ms[('zadd', t)] = sV.bump(mi)

            emit_ln_tree('l1', t, ztc, u1o, ('zadd', t))
            emit_ln_smm('l1', t)
            emit_ln_var('l1', t)
            emit_ln_sd('l1', t)
            emit_ln_fin('l1', t, ztc)
            if t + 1 < NTB:
                emit_trig_rr(t + 1, 0)
                emit_trig_sin(t + 1, 0)

            if t >= 1:
                emit_gw_corr(t - 1, True)
            if t + 1 < NTB:
                emit_trig_cos(t + 1, 0)
                emit_trig_rr(t + 1, 1)
            if t >= 1:
                emit_ln_tree('l2', t - 1, ztp, u1q, ('corr', t - 1, KC - 1))
                gks = ((('pqev', t + 1, 6), ('pqev', t + 1, 7)) if t + 1 < NTB
                       else (('yev', t, 5), ('yev', t, 7)))
                emit_ln_smm('l2', t - 1, gate_keys=gks)
                emit_ln_var('l2', t - 1)
                emit_ln_fin('l2', t - 1, ztp)
                emit_out_dma(t - 1)

            def emit_stat_mms(T, t, jc):
                st_, sp_ = (jc == 0), (jc == JC - 1)
                _wk(T, sS, ms, ('gqv', t, jc // 2))
                T.matmul(psS[:, 0:LT_B], ones_b[:], gb[:, jc * LT_B:(jc + 1) * LT_B],
                         start=st_, stop=sp_, skip_group_check=True)
                mi = T.matmul(psS2[:, 0:LT_B], ones_b[:],
                              gq[:, (jc % 4) * LT_B:(jc % 4 + 1) * LT_B],
                              start=st_, stop=sp_, skip_group_check=True)
                ms[('ssmm', t, jc)] = sT.bump(mi)

            for jcp in range(JC // 2):
                @em.tensor
                def _(T, t=t, jcp=jcp):
                    if jcp == 0:
                        _wk(T, sV, ms, ('l1fin', t))
                        if t >= 1:
                            _wk(T, sS, ms, ('gevp', t - 1, 14))
                    if jcp == 1 and t >= 1:
                        _wk(T, sS, ms, ('gevp', t - 1, 15))
                    for jc in (2 * jcp, 2 * jcp + 1):
                        if jc >= 4:
                            _wk(T, sS, ms, ('gevp', t, (jc - 4) // 2))
                        pt = ps[4 + (jc // 2) % 2]
                        col = (jc % 2) * LT_B
                        for kc in range(KC):
                            mi = T.matmul(pt[:, col:col + LT_B],
                                          wfc8[:, kc * 4096 + jc * 128: kc * 4096 + (jc + 1) * 128],
                                          x1b[t % 2][:, kc * LT_B:(kc + 1) * LT_B],
                                          start=(kc == 0), stop=(kc == KC - 1))
                        ms[('hmm', t, jc)] = sT.bump(mi)
                    if jcp >= 1:
                        for jc in (2 * (jcp - 1), 2 * (jcp - 1) + 1):
                            emit_stat_mms(T, t, jc)

                @em.scalar
                def _(S, t=t, jcp=jcp):
                    jc0 = 2 * jcp
                    _wk(S, sT, ms, ('hmm', t, jc0 + 1))
                    if jcp == 0 and t >= 1:
                        _wk(S, sT, ms, ('gwmm', t - 1, KC - 1))
                    if jcp >= 2:
                        _wk(S, sT, ms, ('ssmm', t, jc0 - 4 + 1))
                    pt = ps[4 + jcp % 2]
                    i = S.activation(gb[:, jc0 * LT_B:(jc0 + 2) * LT_B], pt[:, 0:2 * LT_B], AF.Gelu)
                    ms[('gevp', t, jcp)] = sS.bump(i)
                    gpair = gb[:, jc0 * LT_B:(jc0 + 2) * LT_B]
                    i2 = S.activation(gq[:, (jc0 % 4) * LT_B:(jc0 % 4 + 2) * LT_B],
                                      gpair, AF.Square)
                    ms[('gqv', t, jcp)] = sS.bump(i2)

                if jcp >= 1:
                    @em.vector
                    def _(v, t=t, jcp=jcp):
                        for jc in (2 * (jcp - 1), 2 * (jcp - 1) + 1):
                            _wk(v, sT, ms, ('ssmm', t, jc))
                            gsl = gb[:, jc * LT_B:(jc + 1) * LT_B]
                            i = v.tensor_scalar(gsl, gsl, lnwf[:, jc:jc + 1], None,
                                                ALU.mult, ALU.bypass)
                            ms[('s2a', t, jc)] = sV.bump(i)
                if jcp == 4 and t >= 1:
                    emit_ln_sd('l2', t - 1)
                if t + 1 < NTB:
                    if jcp == 10:
                        emit_trig_sin(t + 1, 1)
                    elif jcp == 13:
                        emit_trig_cos(t + 1, 1)

            @em.tensor
            def _(T, t=t):
                for jc in (JC - 2, JC - 1):
                    emit_stat_mms(T, t, jc)

            @em.vector
            def _(v, t=t):
                for jc in (JC - 2, JC - 1):
                    _wk(v, sT, ms, ('ssmm', t, jc))
                    gsl = gb[:, jc * LT_B:(jc + 1) * LT_B]
                    i = v.tensor_scalar(gsl, gsl, lnwf[:, jc:jc + 1], None, ALU.mult, ALU.bypass)
                    ms[('s2a', t, jc)] = sV.bump(i)


            @em.vector
            def _(v, t=t):
                _wk(v, sT, ms, ('ssmm', t, JC - 1))
                v.tensor_single_scalar(m1[:], psS[:, 0:LT_B], float(2.0 ** -12), ALU.mult)
                v.tensor_tensor(m2[:], m1[:], m1[:], ALU.mult)
                i = v.scalar_tensor_tensor(m2[:], psS2[:, 0:LT_B], float(2.0 ** -12),
                                           m2[:], ALU.mult, ALU.subtract)
                ms[('fvar', t)] = sV.bump(i)

            @em.scalar
            def _(S, t=t):
                _wk(S, sV, ms, ('fvar', t))
                i = S.activation(m2[:], m2[:], AF.Sqrt, bias=eps5[:, 0:1])
                ms[('fsd', t)] = sS.bump(i)

            @em.vector
            def _(v, t=t):
                _wk(v, sS, ms, ('fsd', t))
                ri = v.reciprocal(rsbF[:], m2[:])
                _w(v, sV, sV.bump(ri))
                i = v.tensor_tensor(mrs[:], m1[:], rsbF[:], ALU.mult)
                ms[('ffin', t)] = sV.bump(i)

            em.flush()

        tl = NTB - 1
        emit_gw_corr(tl, False)
        emit_ln_tree('l2', tl, zt[tl % 2], u1q, ('corr', tl, KC - 1))
        emit_ln_smm('l2', tl, gate_keys=(('yev', tl, 5), ('yev', tl, 7)))
        emit_ln_var('l2', tl)
        emit_ln_sd('l2', tl)
        emit_ln_fin('l2', tl, zt[tl % 2])
        emit_out_dma(tl)

        @em.sync
        def _(q):
            q.wait_ge(sO.h, sO.n)
        em.flush()

        Bk.close()
        I.close()

    return nc


def kernel(L0_attn_x, L0_attn_proj_mat, L0_attn_Wq, L0_attn_bq, L0_attn_Wk,
           L0_attn_bk, L0_attn_Wv, L0_attn_bv, L0_attn_Wo, L0_attn_bo,
           L0_ffn_W_fc, L0_ffn_ln_w, L0_ffn_ln_b, L0_ffn_W_proj):
    x = np.asarray(L0_attn_x, dtype=np.float32)
    f32 = lambda a: np.ascontiguousarray(np.asarray(a, dtype=np.float32))
    proj = f32(L0_attn_proj_mat)
    shared = {
        "wqT": f32(np.asarray(L0_attn_Wq, np.float32).T),
        "wkT": f32(np.asarray(L0_attn_Wk, np.float32).T),
        "wvT": f32(np.asarray(L0_attn_Wv, np.float32).T),
        "woT": f32(np.asarray(L0_attn_Wo, np.float32).T),
        "wfcT": f32(np.asarray(L0_ffn_W_fc, np.float32).T),
        "wpjT": f32(np.asarray(L0_ffn_W_proj, np.float32).T),
        "projz": f32(np.hstack([np.vstack([proj, np.zeros((64, 128), np.float32)]),
                                np.vstack([np.zeros((64, 128), np.float32), proj])])),
        "eye": np.eye(128, dtype=np.float32),
        "ones": np.ones((128, 128), np.float32),
        "bqc": f32(np.asarray(L0_attn_bq, np.float32).reshape(8, 128).T),
        "bkc": f32(np.asarray(L0_attn_bk, np.float32).reshape(8, 128).T),
        "boc": f32(np.asarray(L0_attn_bo, np.float32).reshape(8, 128).T),
        "bvb": f32(np.broadcast_to(np.asarray(L0_attn_bv, np.float32), (128, D))),
        "lnwc": f32(np.asarray(L0_ffn_ln_w, np.float32).reshape(32, 128).T),
        "lnbc": f32(np.asarray(L0_ffn_ln_b, np.float32).reshape(32, 128).T),
    }
    nc = build_nc()
    in_maps = []
    for b in range(8):
        m = dict(shared)
        m["xT"] = np.ascontiguousarray(x[b].T)
        in_maps.append(m)
    res = run_bass_kernel_spmd(nc, in_maps, core_ids=list(range(8)))
    out = np.empty((8, L, D), np.float32)
    for b in range(8):
        out[b] = res.results[b]["outT"].T
    return out


# version 11
# speedup vs baseline: 1.0494x; 1.0494x over previous
import numpy as np
import concourse.bass as bass
import concourse.mybir as mybir
from concourse.bass_utils import run_bass_kernel_spmd
from contextlib import ExitStack

F32 = mybir.dt.float32
BF16 = mybir.dt.bfloat16
FP8 = mybir.dt.float8e4
AF = mybir.ActivationFunctionType
ALU = mybir.AluOpType
AX = mybir.AxisListType

D = 1024
L = 4096
H = 16
KC = 8
JC = 32
LT_A, NTA, LCA = 512, 8, 4
LT_B, NTB = 256, 16

B8 = 25.132740020751953
C1X8 = 50.25
C2X8 = float(np.float32(16 * np.pi - 50.25))
EPS_LN = float(np.float32(1e-5))
EPS_G2 = float(np.float32(5e-7))
INV2_23 = float(2.0 ** -23)


class SemC:
    __slots__ = ("h", "n")

    def __init__(self, h):
        self.h = h
        self.n = 0

    def bump(self, instr, k=1):
        instr.then_inc(self.h, k)
        self.n += k
        return self.n


_WAITED = {}


class Emitter:

    ORDER = ("tensor", "scalar", "vector", "sync", "gpsimd")

    def __init__(self, block):
        self.block = block
        self.q = []

    def _add(self, name, f):
        self.q.append((name, f))

    def tensor(self, f):
        self._add("tensor", f)

    def scalar(self, f):
        self._add("scalar", f)

    def vector(self, f):
        self._add("vector", f)

    def sync(self, f):
        self._add("sync", f)

    def gpsimd(self, f):
        self._add("gpsimd", f)

    def flush(self):
        for name in self.ORDER:
            fns = [f for e, f in self.q if e == name]
            if fns:
                getattr(self.block, name)(lambda E, fns=fns: [f(E) for f in fns])
        self.q = []


def _wk(eng, sem, ms, key):
    _w(eng, sem, ms.get(key, 0))


def _w(eng, sem, val):
    if not val or val <= 0:
        return
    k = (type(eng).__name__, id(sem))
    if _WAITED.get(k, 0) >= val:
        return
    _WAITED[k] = val
    eng.wait_ge(sem.h, val)


def build_nc(phases='full', ms_preset=None, _ms_out=None, _two_pass=True):
    if _two_pass:
        holder = []
        build_nc(phases, ms_preset=None, _ms_out=holder, _two_pass=False)
        return build_nc(phases, ms_preset=holder[0], _ms_out=None, _two_pass=False)
    _WAITED.clear()
    nc = bass.Bass("TRN2", target_bir_lowering=False, detect_race_conditions=False)

    def din(name, shape):
        return nc.dram_tensor(name, shape, F32, kind="ExternalInput").ap()

    x_d = din("xT", [D, L])
    wq_d = din("wqT", [D, D])
    wk_d = din("wkT", [D, D])
    wv_d = din("wvT", [D, D])
    wo_d = din("woT", [D, D])
    wfc_d = din("wfcT", [D, 4096])
    wpj_d = din("wpjT", [4096, D])
    projz_d = din("projz", [128, 256])
    eye_d = din("eye", [128, 128])
    ones_d = din("ones", [128, 128])
    bqc_d = din("bqc", [128, 8])
    bkc_d = din("bkc", [128, 8])
    boc_d = din("boc", [128, 8])
    bvb_d = din("bvb", [128, D])
    lnw_d = din("lnwc", [128, 32])
    lnb_d = din("lnbc", [128, 32])
    out_d = nc.dram_tensor("outT", [D, L], F32, kind="ExternalOutput").ap()

    perm = ExitStack()

    def sb(name, shape, dt):
        return perm.enter_context(nc.sbuf_tensor(name, shape, dt))

    projz_b = sb("projz_b", [128, 256], BF16)
    eye_b = sb("eye_b", [128, 128], BF16)
    ones_f = sb("ones_f", [128, 128], F32)
    ones_b = sb("ones_b", [128, 128], BF16)
    bqc = sb("bqc_s", [128, 8], F32)
    bkc = sb("bkc_s", [128, 8], F32)
    boc = sb("boc_s", [128, 8], F32)
    bvb = sb("bvb_s", [128, D], F32)
    lnwf = sb("lnwf", [128, 32], F32)
    wcol_b = sb("wcol_b", [128, 32], BF16)
    bcol_b = sb("bcol_b", [128, 32], BF16)
    eps5 = sb("eps5", [128, 1], F32)
    eps57 = sb("eps57", [128, 1], F32)
    kv_acc = sb("kv_acc", [128, 2048], F32)

    ps = [perm.enter_context(nc.psum_tensor(f"ps{i}", [128, 512], F32)) for i in range(6)]
    psS = perm.enter_context(nc.psum_tensor("psS", [128, 512], F32))

    sDL = SemC(perm.enter_context(nc.semaphore("sDL")))
    sDG = SemC(perm.enter_context(nc.semaphore("sDG")))
    sXA = [SemC(perm.enter_context(nc.semaphore(f"sXA{i}"))) for i in range(2)]
    sXF = [SemC(perm.enter_context(nc.semaphore(f"sXF{i}"))) for i in range(2)]
    sXB = [SemC(perm.enter_context(nc.semaphore(f"sXB{i}"))) for i in range(2)]
    sWS = [SemC(perm.enter_context(nc.semaphore(f"sWS{i}"))) for i in range(2)]
    sT = SemC(perm.enter_context(nc.semaphore("sT")))
    sS = SemC(perm.enter_context(nc.semaphore("sS")))
    sV = SemC(perm.enter_context(nc.semaphore("sV")))
    sO = SemC(perm.enter_context(nc.semaphore("sO")))

    ms = dict(ms_preset) if ms_preset else {}
    if _ms_out is not None:
        _ms_out.append(ms)

    with nc.Block() as block:
        em = Emitter(block)
        @em.sync
        def _(q):
            for dst, src in ((ones_f, ones_d), (bqc, bqc_d), (bkc, bkc_d),
                             (boc, boc_d), (bvb, bvb_d), (lnwf, lnw_d)):
                sDL.bump(q.dma_start(out=dst[:], in_=src), 16)
            ms['setup_sync'] = sDL.n

        @em.gpsimd
        def _(g):
            for dst, src in ((projz_b, projz_d), (eye_b, eye_d), (ones_b, ones_d),
                             (wcol_b, lnw_d), (bcol_b, lnb_d)):
                sDG.bump(g.dma_start(out=dst[:], in_=src), 16)
            ms['setup_g'] = sDG.n

        @em.vector
        def _(v):
            v.memset(eps5[:], EPS_LN)
            v.memset(eps57[:], EPS_G2)
            mi = v.memset(kv_acc[:], 0.0)
            sV.bump(mi)
            ms['init_v'] = sV.n
        em.flush()

        Iq = ExitStack()

        def sbi(name, shape, dt):
            return Iq.enter_context(nc.sbuf_tensor(name, shape, dt, side="right"))

        wst = [sbi(f"wst{i}", [128, 4096], F32) for i in range(2)]
        cnq = sbi("cnq", [128, 4096], F32)
        gfc = sbi("gfc", [128, 1], F32)
        gpj = sbi("gpj", [128, 1], F32)
        gtmp = sbi("gtmp", [128, 1], F32)
        gsum = sbi("gsum", [1, 1], F32)
        hgf = sbi("hgf", [128, 1], F32)
        hgp = sbi("hgp", [128, 1], F32)
        hgnf = sbi("hgnf", [128, 1], F32)
        hgnp = sbi("hgnp", [128, 1], F32)
        kvev = sbi("kvev", [128, 2048], BF16)
        ps8 = Iq.enter_context(nc.psum_tensor("ps8", [128, 512], BF16))

        gam_tiles = [('fc', i) for i in range(KC)] + [('pj', i) for i in range(JC)]

        @em.vector
        def _(v):
            v.memset(gfc[:], 0.0)
            mi = v.memset(gpj[:], 0.0)
            sV.bump(mi)
            ms['gz0'] = sV.n

        def emit_gamma_tile(i):
            w, idx = gam_tiles[i]

            @em.sync
            def _(q, i=i, w=w, idx=idx):
                if i >= 2:
                    _wk(q, sV, ms, ('gred', i - 2))
                if w == 'fc':
                    src = wfc_d[idx * 128:(idx + 1) * 128, :]
                    dst = wst[i % 2][:, 0:4096]
                else:
                    src = wpj_d[idx * 128:(idx + 1) * 128, :]
                    dst = wst[i % 2][:, 0:1024]
                sWS[i % 2].bump(q.dma_start(out=dst, in_=src), 16)
                ms[('gld', i)] = sWS[i % 2].n

            @em.vector
            def _(v, i=i, w=w, idx=idx):
                if i == 0:
                    _wk(v, sV, ms, 'gz0')
                _wk(v, sWS[i % 2], ms, ('gld', i))
                wid = 4096 if w == 'fc' else 1024
                ri = v.reduce_sum(gtmp[:], wst[i % 2][:, 0:wid], axis=AX.X,
                                  apply_absolute_value=True)
                _w(v, sV, sV.bump(ri))
                acc = gfc if w == 'fc' else gpj
                i2 = v.tensor_tensor(acc[:], acc[:], gtmp[:], ALU.add)
                ms[('gred', i)] = sV.bump(i2)

        A = ExitStack()

        def sba(name, shape, dt):
            return A.enter_context(nc.sbuf_tensor(name, shape, dt))

        wk_b = sba("wk_b", [128, KC * D], BF16)
        wv_b = sba("wv_b", [128, KC * D], BF16)
        xa = [sba(f"xa{i}", [128, KC * LT_A], BF16) for i in range(2)]
        k_sb = sba("k_sb", [128, KC * LT_A], BF16)
        p8k = sba("p8k", [128, LCA * 2048], BF16)
        kp = sba("kp", [128, LCA * H * 256], BF16)
        vtm = sba("vtm", [128, LCA * D], BF16)
        tg1 = [sba(f"tg1{i}", [128, 2048], BF16) for i in range(2)]
        tg2 = [sba(f"tg2{i}", [128, 2048], BF16) for i in range(2)]

        @em.gpsimd
        def _(g):
            for wb, wd in ((wk_b, wk_d), (wv_b, wv_d)):
                src = wd.rearrange("(c p) o -> p c o", p=128)
                dst = wb[:].rearrange("p (c o) -> p c o", c=KC)
                sDG.bump(g.dma_start(out=dst, in_=src), 16)
            ms['wkv'] = sDG.n
            src = x_d[:, 0:LT_A].rearrange("(c p) l -> p c l", p=128)
            dst = xa[0][:].rearrange("p (c l) -> p c l", c=KC)
            sXA[0].bump(g.dma_start(out=dst, in_=src), 16)
            ms[('xa', 0)] = sXA[0].n

        for t in range(NTA):
            if t + 1 < NTA:
                @em.gpsimd
                def _(g, t=t):
                    if t >= 1:
                        _w(g, sT, ms.get(('vmm', t - 1, 2 * LCA - 1)) or ms.get(('kmm', t - 1, KC - 1), 0))
                    src = x_d[:, (t + 1) * LT_A:(t + 2) * LT_A].rearrange("(c p) l -> p c l", p=128)
                    dst = xa[(t + 1) % 2][:].rearrange("p (c l) -> p c l", c=KC)
                    sXA[(t + 1) % 2].bump(g.dma_start(out=dst, in_=src), 16)
                    ms[('xa', t + 1)] = sXA[(t + 1) % 2].n

            xt = xa[t % 2]

            for ocp in range(KC // 2):
                @em.tensor
                def _(T, t=t, xt=xt, ocp=ocp):
                    if ocp == 0:
                        _wk(T, sXA[t % 2], ms, ('xa', t))
                        if t == 0:
                            _wk(T, sDG, ms, 'wkv')
                            _wk(T, sV, ms, 'init_v')
                    for oc in (2 * ocp, 2 * ocp + 1):
                        gi = t * KC + oc
                        if gi >= 4:
                            pt_, po_ = divmod(gi - 4, KC)
                            _wk(T, sS, ms, ('kev', pt_, po_))
                        if t >= 1 and oc < 4:
                            _wk(T, sV, ms, ('vev', t - 1, 4 + oc))
                        pt = ps[oc % 4]
                        for kc in range(KC):
                            mi = T.matmul(pt[:], wk_b[:, kc * D + oc * 128: kc * D + (oc + 1) * 128],
                                          xt[:, kc * LT_A:(kc + 1) * LT_A],
                                          start=(kc == 0), stop=(kc == KC - 1))
                        ms[('kmm', t, oc)] = sT.bump(mi)

                @em.scalar
                def _(S, t=t, ocp=ocp):
                    if t == 0 and ocp == 0:
                        _wk(S, sDL, ms, 'setup_sync')
                    for oc in (2 * ocp, 2 * ocp + 1):
                        _wk(S, sT, ms, ('kmm', t, oc))
                        i = S.activation(k_sb[:, oc * LT_A:(oc + 1) * LT_A], ps[oc % 4][:],
                                         AF.Identity, bias=bkc[:, oc:oc + 1])
                        ms[('kev', t, oc)] = sS.bump(i)

            for lc in (range(LCA) if phases not in ('A1',) else ()):
                for g4p in range(2):
                    @em.tensor
                    def _(T, t=t, lc=lc, g4p=g4p):
                        if lc == 0 and g4p == 0:
                            _wk(T, sS, ms, ('kev', t, KC - 1))
                        for h in range(8 * g4p, 8 * g4p + 8):
                            g4 = h // 4
                            pg = (t * LCA + lc) * 4 + g4
                            if h % 4 == 0:
                                if pg >= 2:
                                    pgt, pgr = divmod(pg - 2, 4 * LCA)
                                    pglc, pgg = divmod(pgr, 4)
                                    _wk(T, sS, ms, ('pev', pgt, pglc, pgg))
                                if t >= 1 and lc == 0 and g4 < 2 and ('kvadd', t - 1, 6 + g4) in ms:
                                    _wk(T, sV, ms, ('kvadd', t - 1, 6 + g4))
                            bank = ps[4 + (g4 % 2)]
                            half = h % 2
                            oc = h // 2
                            lhsT = k_sb[:, oc * LT_A + lc * 128: oc * LT_A + (lc + 1) * 128]
                            rhs = projz_b[:, half * 128:(half + 1) * 128]
                            mi = T.matmul(bank[:, (h % 4) * 128:(h % 4 + 1) * 128], lhsT, rhs,
                                          start=True, stop=True, skip_group_check=True)
                            if h % 4 == 3:
                                ms[('pmm', t, lc, g4)] = sT.bump(mi)

                    @em.scalar
                    def _(S, t=t, lc=lc, g4p=g4p):
                        for g4 in (2 * g4p, 2 * g4p + 1):
                            _wk(S, sT, ms, ('pmm', t, lc, g4))
                            if g4 == 0 and t >= 1:
                                _wk(S, sV, ms, ('trg1', t - 1, lc))
                            i = S.activation(p8k[:, lc * 2048 + g4 * 512: lc * 2048 + (g4 + 1) * 512],
                                             ps[4 + (g4 % 2)][:], AF.Identity)
                            ms[('pev', t, lc, g4)] = sS.bump(i)

                if phases == 'A2a':
                    ms[('trg1', t, lc)] = None
                    ms[('sh', t, lc)] = None
                    ms[('cos', t, lc)] = None
                    continue

                @em.vector
                def _(v, t=t, lc=lc):
                    _wk(v, sS, ms, ('pev', t, lc, 3))
                    p8 = p8k[:, lc * 2048:(lc + 1) * 2048]
                    g1, g2 = tg1[lc % 2], tg2[lc % 2]
                    v.tensor_single_scalar(g1[:], p8, -B8, ALU.is_lt)
                    v.scalar_tensor_tensor(g2[:], p8, B8, g1[:], ALU.is_gt, ALU.subtract)
                    v.scalar_tensor_tensor(g1[:], g2[:], -C1X8, p8, ALU.mult, ALU.add)
                    v.scalar_tensor_tensor(g2[:], g2[:], -C2X8, g1[:], ALU.mult, ALU.add)
                    i = v.tensor_scalar(g1[:], g2[:], -B8, B8, ALU.max, ALU.min)
                    ms[('trg1', t, lc)] = sV.bump(i)

                for lcd in ([lc - 1] if lc >= 1 else []) + ([LCA - 1] if lc == LCA - 1 else []):
                    @em.scalar
                    def _(S, t=t, lcd=lcd):
                        _wk(S, sV, ms, ('trg1', t, lcd))
                        if lcd == 0 and t >= 1 and ('kvmm', t - 1, KC - 1) in ms:
                            _wk(S, sT, ms, ('kvmm', t - 1, KC - 1))
                        base = lcd * H * 256
                        W = LCA * H * 256
                        sin_out = bass.AP(kp, base + 128, [[W, 128], [256, H], [1, 128]])
                        S.activation(sin_out, tg1[lcd % 2][:], AF.Sin, scale=0.125)
                        i = S.activation(tg2[lcd % 2][:], tg1[lcd % 2][:], AF.Sin, scale=0.0625)
                        ms[('sh', t, lcd)] = sS.bump(i)

                    @em.vector
                    def _(v, t=t, lcd=lcd):
                        _wk(v, sS, ms, ('sh', t, lcd))
                        g2 = tg2[lcd % 2]
                        v.tensor_tensor(g2[:], g2[:], g2[:], ALU.mult)
                        base = lcd * H * 256
                        W = LCA * H * 256
                        cos_out = bass.AP(kp, base, [[W, 128], [256, H], [1, 128]])
                        i = v.tensor_scalar(cos_out, g2[:], -2.0, 1.0, ALU.mult, ALU.add)
                        ms[('cos', t, lcd)] = sV.bump(i)

            for lc in (range(LCA) if phases not in ('A1', 'A2', 'A2a') else ()):
                @em.tensor
                def _(T, t=t, xt=xt, lc=lc):
                    for half in range(2):
                        idx = lc * 2 + half
                        gi = t * 2 * LCA + idx
                        if gi >= 4:
                            vt_, vi_ = divmod(gi - 4, 2 * LCA)
                            _wk(T, sV, ms, ('vev', vt_, vi_))
                        if idx < 4:
                            _wk(T, sS, ms, ('kev', t, 4 + idx))
                        pt = ps[idx % 4]
                        for kc in range(KC):
                            mi = T.matmul(pt[:],
                                          xt[:, kc * LT_A + lc * 128: kc * LT_A + (lc + 1) * 128],
                                          wv_b[:, kc * D + half * 512: kc * D + (half + 1) * 512],
                                          start=(kc == 0), stop=(kc == KC - 1))
                        ms[('vmm', t, idx)] = sT.bump(mi)

                @em.vector
                def _(v, t=t, lc=lc):
                    if t == 0 and lc == 0:
                        _wk(v, sDL, ms, 'setup_sync')
                    for half in range(2):
                        idx = lc * 2 + half
                        _wk(v, sT, ms, ('vmm', t, idx))
                        i = v.scalar_tensor_tensor(
                            vtm[:, lc * D + half * 512: lc * D + (half + 1) * 512],
                            ps[idx % 4][:], 0.0, bvb[:, half * 512:(half + 1) * 512],
                            ALU.add, ALU.add)
                        ms[('vev', t, idx)] = sV.bump(i)

            for pair in (range(KC) if phases not in ('A1', 'A2', 'A2a', 'A3') else ()):
                @em.tensor
                def _(T, t=t, pair=pair):
                    if pair == 0:
                        _wk(T, sV, ms, ('cos', t, LCA - 1))
                        _wk(T, sV, ms, ('vev', t, 2 * LCA - 1))
                    gi = t * KC + pair
                    if gi >= 2:
                        kt_, kp_ = divmod(gi - 2, KC)
                        _wk(T, sV, ms, ('kvadd', kt_, kp_))
                    pt = ps[4 + pair % 2]
                    for lc in range(LCA):
                        st_, sp_ = (lc == 0), (lc == LCA - 1)
                        T.matmul(pt[0:64, 0:256],
                                 vtm[:, lc * D + pair * 128: lc * D + pair * 128 + 64],
                                 kp[:, lc * H * 256 + (2 * pair) * 256: lc * H * 256 + (2 * pair + 1) * 256],
                                 start=st_, stop=sp_, tile_position=(0, 0), skip_group_check=True)
                        mi = T.matmul(pt[64:128, 0:256],
                                      vtm[:, lc * D + pair * 128 + 64: lc * D + (pair + 1) * 128],
                                      kp[:, lc * H * 256 + (2 * pair + 1) * 256: lc * H * 256 + (2 * pair + 2) * 256],
                                      start=st_, stop=sp_, tile_position=(0, 64), skip_group_check=True)
                    ms[('kvmm', t, pair)] = sT.bump(mi)

                @em.vector
                def _(v, t=t, pair=pair):
                    _wk(v, sT, ms, ('kvmm', t, pair))
                    sl = kv_acc[:, pair * 256:(pair + 1) * 256]
                    i = v.tensor_tensor(sl, sl, ps[4 + pair % 2][:, 0:256], ALU.add)
                    ms[('kvadd', t, pair)] = sV.bump(i)

            if phases == 'full':
                for gi in range(5 * t, 5 * t + 5):
                    emit_gamma_tile(gi)
            em.flush()

        ms['A_T_end'] = sT.n
        ms['A_V_end'] = ms.get(('kvadd', NTA - 1, KC - 1), sV.n)
        ms['A_S_end'] = sS.n

        if phases in ('A', 'A1', 'A2', 'A2a', 'A3'):
            @em.sync
            def _(q):
                if phases == 'A':
                    _wk(q, sV, ms, 'A_V_end')
                    sO.bump(q.dma_start(out=out_d[0:128, 0:2048], in_=kv_acc[:]), 16)
                q.nop()

            @em.gpsimd
            def _(g):
                if phases == 'A1':
                    _wk(g, sS, ms, ('kev', NTA - 1, KC - 1))
                    sO.bump(g.dma_start(out=out_d[0:128, 0:KC * LT_A], in_=k_sb[:]), 16)
                elif phases == 'A2':
                    _wk(g, sV, ms, ('cos', NTA - 1, LCA - 1))
                    sO.bump(g.dma_start(out=out_d[0:128, 0:4096], in_=kp[:, 0:4096]), 16)
                elif phases == 'A2a':
                    _wk(g, sS, ms, ('pev', NTA - 1, LCA - 1, 3))
                    sO.bump(g.dma_start(out=out_d[0:128, 0:4096], in_=p8k[:, 0:4096]), 16)
                elif phases == 'A3':
                    _wk(g, sV, ms, ('vev', NTA - 1, 2 * LCA - 1))
                    sO.bump(g.dma_start(out=out_d[0:128, 0:LCA * D], in_=vtm[:]), 16)
                g.wait_ge(sO.h, sO.n)
            em.flush()
            A.close()
            perm.pop_all()
            return nc
        A.close()

        I = ExitStack()
        wq_b = I.enter_context(nc.sbuf_tensor("wq_b", [128, KC * D], BF16))
        wo_b = I.enter_context(nc.sbuf_tensor("wo_b", [128, KC * D], BF16))
        wfc8 = I.enter_context(nc.sbuf_tensor("wfc8", [128, KC * 4096], FP8))
        wpj8 = I.enter_context(nc.sbuf_tensor("wpj8", [128, JC * D], FP8))
        kvT = I.enter_context(nc.sbuf_tensor("kvT", [128, 2048], BF16))
        ucol = I.enter_context(nc.sbuf_tensor("ucol", [128, 8], F32))
        ncol = I.enter_context(nc.sbuf_tensor("ncol", [128, 8], F32))

        if phases == 'AI1':
            @em.gpsimd
            def _(g):
                _wk(g, sV, ms, ('gred', len(gam_tiles) - 1))
                sO.bump(g.dma_start(out=out_d[0:128, 0:1], in_=gfc[:], allow_slow_non_contiguous=True), 16)
                sO.bump(g.dma_start(out=out_d[0:128, 1:2], in_=gpj[:], allow_slow_non_contiguous=True), 16)
                g.wait_ge(sO.h, sO.n)
            em.flush()
            Iq.close()
            return nc

        for wname, acc, hg, hgn in (('fc', gfc, hgf, hgnf), ('pj', gpj, hgp, hgnp)):
            @em.tensor
            def _(T, acc=acc, wname=wname):
                _wk(T, sV, ms, ('gred', len(gam_tiles) - 1))
                _wk(T, sDL, ms, 'setup_sync')
                if wname == 'pj':
                    _wk(T, sS, ms, ('gfin_s', 'fc'))
                mi = T.matmul(psS[0:1, 0:1], acc[:], ones_f[:, 0:1], start=True, stop=True)
                ms[('gfin_t1', wname)] = sT.bump(mi)

            @em.scalar
            def _(S, wname=wname):
                _wk(S, sT, ms, ('gfin_t1', wname))
                i = S.activation(gsum[:], psS[0:1, 0:1], AF.Identity)
                ms[('gfin_s1', wname)] = sS.bump(i)

            @em.tensor
            def _(T, wname=wname):
                _wk(T, sS, ms, ('gfin_s1', wname))
                mi = T.matmul(psS[:, 0:1], ones_f[0:1, :], gsum[:], start=True, stop=True)
                ms[('gfin_t2', wname)] = sT.bump(mi)

            @em.scalar
            def _(S, hg=hg, wname=wname):
                _wk(S, sT, ms, ('gfin_t2', wname))
                i = S.activation(hg[:], psS[:, 0:1], AF.Identity, scale=INV2_23,
                                 bias=eps57[:, 0:1])
                ms[('gfin_s', wname)] = sS.bump(i)

            @em.vector
            def _(v, hg=hg, hgn=hgn, wname=wname):
                _wk(v, sS, ms, ('gfin_s', wname))
                i = v.tensor_single_scalar(hgn[:], hg[:], -1.0, ALU.mult)
                ms[('gfin_v', wname)] = sV.bump(i)

        for i, (w, idx) in enumerate(gam_tiles):
            @em.sync
            def _(q, i=i, w=w, idx=idx):
                if i == 0:
                    _wk(q, sV, ms, ('gfin_v', 'pj'))
                if i >= 2:
                    _wk(q, sV, ms, ('qdone', i - 2))
                if w == 'fc':
                    src = wfc_d[idx * 128:(idx + 1) * 128, :]
                    dst = wst[i % 2][:, 0:4096]
                else:
                    src = wpj_d[idx * 128:(idx + 1) * 128, :]
                    dst = wst[i % 2][:, 0:1024]
                sWS[i % 2].bump(q.dma_start(out=dst, in_=src), 16)
                ms[('qld', i)] = sWS[i % 2].n

            @em.vector
            def _(v, i=i, w=w, idx=idx):
                _wk(v, sWS[i % 2], ms, ('qld', i))
                wid = 4096 if w == 'fc' else 1024
                hg = hgf if w == 'fc' else hgp
                hgn = hgnf if w == 'fc' else hgnp
                wt = wst[i % 2][:, 0:wid]
                v.tensor_single_scalar(cnq[:, 0:wid], wt, hgn[:, 0:1], ALU.is_lt)
                if w == 'fc':
                    out = wfc8[:, idx * 4096:(idx + 1) * 4096]
                else:
                    out = wpj8[:, idx * D:(idx + 1) * D]
                i2 = v.scalar_tensor_tensor(out, wt, hg[:, 0:1], cnq[:, 0:wid],
                                            ALU.is_gt, ALU.subtract)
                ms[('qdone', i)] = sV.bump(i2)


        for which, rhs_cols, outcol, scl in (('u', wcol_b, ucol, 1.0), ('c', bcol_b, ncol, -1.0)):
            for oc in range(KC):
                @em.tensor
                def _(T, rhs_cols=rhs_cols, oc=oc, which=which):
                    if which == 'u' and oc == 0:
                        _wk(T, sV, ms, ('qdone', len(gam_tiles) - 1))
                        _wk(T, sS, ms, ('gfin_s', 'pj'))
                        _wk(T, sDG, ms, 'setup_g')
                    else:
                        prev = ('c', oc - 1) if which == 'c' and oc > 0 else \
                               (('u', KC - 1) if which == 'c' else ('u', oc - 1))
                        _wk(T, sS, ms, ('ucev',) + prev)
                    for jc in range(JC):
                        mi = T.matmul(psS[:, 0:1],
                                      wpj8[:, jc * D + oc * 128: jc * D + (oc + 1) * 128],
                                      rhs_cols[:, jc:jc + 1],
                                      start=(jc == 0), stop=(jc == JC - 1))
                    ms[('ucmm', which, oc)] = sT.bump(mi)

                @em.scalar
                def _(S, outcol=outcol, oc=oc, which=which, scl=scl):
                    _wk(S, sT, ms, ('ucmm', which, oc))
                    i = S.activation(outcol[:, oc:oc + 1], psS[:, 0:1], AF.Identity, scale=scl)
                    ms[('ucev', which, oc)] = sS.bump(i)

        @em.scalar
        def _(S):
            _wk(S, sV, ms, 'A_V_end')
            i = S.activation(kvev[:], kv_acc[:], AF.Identity, scale=float(2.0 ** -7))
            ms['kvev'] = sS.bump(i)

        for pair in range(KC):
            for mc in range(2):
                idx = pair * 2 + mc

                @em.tensor
                def _(T, pair=pair, mc=mc, idx=idx):
                    if idx == 0:
                        _wk(T, sDG, ms, 'setup_g')
                    _w(T, sS, (ms.get('kvev', 0) if idx == 0 else ms.get(('kvtev', idx - 1), 0)))
                    src = kvev[:, pair * 256 + mc * 128: pair * 256 + (mc + 1) * 128]
                    mi = T.transpose(ps8[:, 0:128], src, eye_b[:])
                    ms[('kvtmm', idx)] = sT.bump(mi)

                @em.scalar
                def _(S, pair=pair, mc=mc, idx=idx):
                    _wk(S, sT, ms, ('kvtmm', idx))
                    he = 2 * pair
                    ho = 2 * pair + 1
                    S.activation(kvT[:, (he * 2 + mc) * 64:(he * 2 + mc + 1) * 64],
                                 ps8[:, 0:64], AF.Identity)
                    i = S.activation(kvT[:, (ho * 2 + mc) * 64:(ho * 2 + mc + 1) * 64],
                                     ps8[:, 64:128], AF.Identity)
                    ms[('kvtev', idx)] = sS.bump(i)


        @em.gpsimd
        def _(g):
            _wk(g, sT, ms, 'A_T_end')
            _wk(g, sS, ms, 'A_S_end')
            _wk(g, sV, ms, 'A_V_end')
            for wb, wd in ((wq_b, wq_d), (wo_b, wo_d)):
                src = wd.rearrange("(c p) o -> p c o", p=128)
                dst = wb[:].rearrange("p (c o) -> p c o", c=KC)
                sDG.bump(g.dma_start(out=dst, in_=src), 16)
            ms['wqwo'] = sDG.n

        if phases == 'AI':
            @em.gpsimd
            def _(g):
                _wk(g, sS, ms, ('kvtev', 2 * KC - 1))
                _wk(g, sV, ms, ('qdone', len(gam_tiles) - 1))
                _wk(g, sS, ms, ('ucev', 'c', KC - 1))
                sO.bump(g.dma_start(out=out_d[0:128, 0:2048], in_=kvT[:]), 16)
                sO.bump(g.dma_start(out=out_d[0:128, 2048:2056], in_=ucol[:]), 16)
                sO.bump(g.dma_start(out=out_d[0:128, 2056:2064], in_=ncol[:]), 16)
                sO.bump(g.dma_start(out=out_d[0:128, 2064:2065], in_=hgf[:], allow_slow_non_contiguous=True), 16)
                sO.bump(g.dma_start(out=out_d[0:128, 2065:2066], in_=hgp[:], allow_slow_non_contiguous=True), 16)
                sO.bump(g.dma_start(out=out_d[0:128, 2066:2067], in_=gfc[:], allow_slow_non_contiguous=True), 16)
                sO.bump(g.dma_start(out=out_d[128:256, 0:2048], in_=kvev[:]), 16)
                sO.bump(g.dma_start(out=out_d[256:384, 0:4096], in_=wpj8[:, 0:4096]), 16)
                g.wait_ge(sO.h, sO.n)
            em.flush()
            Iq.close()
            I.pop_all()
            perm.pop_all()
            return nc
        em.flush()
        Iq.close()

        Bk = ExitStack()

        def sbb(name, shape, dt):
            return Bk.enter_context(nc.sbuf_tensor(name, shape, dt))

        xf = sbb("xf", [128, KC * LT_B], F32)
        xb = sbb("xb", [128, KC * LT_B], BF16)
        u1q = sbb("u1q", [128, KC * LT_B], BF16)
        u1o = sbb("u1o", [128, KC * LT_B], BF16)
        p8q = sbb("p8q", [128, H * LT_B], BF16)
        tq1 = sbb("tq1", [128, H * LT_B // 2], BF16)
        tq2 = sbb("tq2", [128, H * LT_B // 2], BF16)
        qps = sbb("qps", [128, H * LT_B], BF16)
        qpc = p8q
        zt = [sbb(f"zt{i}", [128, KC * LT_B], F32) for i in range(2)]
        x1b = [sbb(f"x1b{i}", [128, KC * LT_B], BF16) for i in range(2)]
        gb = sbb("gb", [128, JC * LT_B], BF16)
        rsbF = sbb("rsbF", [128, LT_B], F32)
        gq = sbb("gq", [128, 4 * LT_B], BF16)
        s1f = sbb("s1f", [128, LT_B], F32)
        s2b = sbb("s2b", [128, LT_B], BF16)
        psS2 = Bk.enter_context(nc.psum_tensor("psS2", [128, 512], F32))
        m1 = sbb("m1", [128, LT_B], F32)
        m2 = sbb("m2", [128, LT_B], F32)
        rsb = sbb("rsb", [128, LT_B], F32)
        mrs = sbb("mrs", [128, LT_B], F32)
        t2b = sbb("t2b", [128, LT_B], F32)
        t1b = m1

        @em.sync
        def _(q):
            _wk(q, sV, ms, ('qdone', len(gam_tiles) - 1))
            _wk(q, sS, ms, ('kvtev', 2 * KC - 1))
            src = x_d[:, 0:LT_B].rearrange("(c p) l -> p c l", p=128)
            dst = xf[:].rearrange("p (c l) -> p c l", c=KC)
            sXF[0].bump(q.dma_start(out=dst, in_=src), 16)
            ms[('xf', 0)] = sXF[0].n

        @em.gpsimd
        def _(g):
            _wk(g, sV, ms, ('qdone', len(gam_tiles) - 1))
            _wk(g, sS, ms, ('kvtev', 2 * KC - 1))
            src = x_d[:, 0:LT_B].rearrange("(c p) l -> p c l", p=128)
            dst = xb[:].rearrange("p (c l) -> p c l", c=KC)
            sXB[0].bump(g.dma_start(out=dst, in_=src), 16)
            ms[('xbB', 0)] = sXB[0].n

        def emit_xb_load(t):
            @em.gpsimd
            def _(g, t=t):
                _wk(g, sT, ms, ('qmm', t - 1, KC - 1))
                src = x_d[:, t * LT_B:(t + 1) * LT_B].rearrange("(c p) l -> p c l", p=128)
                dst = xb[:].rearrange("p (c l) -> p c l", c=KC)
                sXB[t % 2].bump(g.dma_start(out=dst, in_=src), 16)
                ms[('xbB', t)] = sXB[t % 2].n

        def emit_ln_tree(tag, t, src, sqdst, pre_key):
            @em.vector
            def _(v, t=t):
                _wk(v, sV, ms, pre_key)
                for oc in range(KC):
                    sl = src[:, oc * LT_B:(oc + 1) * LT_B]
                    v.tensor_tensor(sqdst[:, oc * LT_B:(oc + 1) * LT_B], sl, sl, ALU.mult)
                v.tensor_tensor(s1f[:], src[:, 0:LT_B], src[:, LT_B:2 * LT_B], ALU.add)
                for oc in range(2, KC):
                    v.tensor_tensor(s1f[:], s1f[:], src[:, oc * LT_B:(oc + 1) * LT_B], ALU.add)
                v.tensor_tensor(s2b[:], sqdst[:, 0:LT_B], sqdst[:, LT_B:2 * LT_B], ALU.add)
                for oc in range(2, KC):
                    mi = v.tensor_tensor(s2b[:], s2b[:], sqdst[:, oc * LT_B:(oc + 1) * LT_B], ALU.add)
                ms[(f'{tag}tree', t)] = sV.bump(mi)

        def emit_ln_smm(tag, t, gate_keys=()):
            @em.tensor
            def _(T, t=t):
                _wk(T, sV, ms, (f'{tag}tree', t))
                for gk in gate_keys:
                    _wk(T, sS, ms, gk)
                if tag == 'l1':
                    if t == 0:
                        _wk(T, sDL, ms, 'setup_sync')
                        _wk(T, sDG, ms, 'setup_g')
                    d1, d2 = psS[:, 0:LT_B], psS[:, LT_B:2 * LT_B]
                else:
                    if t >= 1:
                        _wk(T, sV, ms, ('l2var', t - 1))
                    d1, d2 = ps[0][:, 0:LT_B], ps[1][:, 0:LT_B]
                T.matmul(d1, ones_f[:], s1f[:], start=True, stop=True,
                         skip_group_check=True)
                mi = T.matmul(d2, ones_b[:], s2b[:], start=True, stop=True,
                              skip_group_check=True)
                ms[(f'{tag}smm', t)] = sT.bump(mi)

        def emit_ln_var(tag, t):
            @em.vector
            def _(v, t=t):
                _wk(v, sT, ms, (f'{tag}smm', t))
                if tag == 'l1':
                    d1, d2 = psS[:, 0:LT_B], psS[:, LT_B:2 * LT_B]
                else:
                    d1, d2 = ps[0][:, 0:LT_B], ps[1][:, 0:LT_B]
                v.tensor_single_scalar(m1[:], d1, float(2.0 ** -10), ALU.mult)
                v.tensor_tensor(m2[:], m1[:], m1[:], ALU.mult)
                i = v.scalar_tensor_tensor(m2[:], d2, float(2.0 ** -10),
                                           m2[:], ALU.mult, ALU.subtract)
                ms[(f'{tag}var', t)] = sV.bump(i)

        def emit_ln_sd(tag, t):
            @em.scalar
            def _(S, t=t):
                _wk(S, sV, ms, (f'{tag}var', t))
                i = S.activation(m2[:], m2[:], AF.Sqrt, bias=eps5[:, 0:1])
                ms[(f'{tag}sd', t)] = sS.bump(i)

        def emit_ln_fin(tag, t, src):
            @em.vector
            def _(v, t=t):
                _wk(v, sS, ms, (f'{tag}sd', t))
                ri = v.reciprocal(rsb[:], m2[:])
                _w(v, sV, sV.bump(ri))
                if tag == 'l1' and t >= 2:
                    _wk(v, sT, ms, ('hmm', t - 2, JC - 1))
                for oc in range(KC):
                    sl = src[:, oc * LT_B:(oc + 1) * LT_B]
                    v.tensor_tensor(sl, sl, m1[:], ALU.subtract)
                    if tag == 'l1':
                        mi = v.tensor_tensor(x1b[t % 2][:, oc * LT_B:(oc + 1) * LT_B],
                                             sl, rsb[:], ALU.mult)
                    else:
                        mi = v.tensor_tensor(sl, sl, rsb[:], ALU.mult)
                ms[(f'{tag}fin', t)] = sV.bump(mi)

        def emit_out_dma(t):
            @em.sync
            def _(q, t=t):
                _wk(q, sV, ms, ('l2fin', t))
                src = zt[t % 2][:].rearrange("p (c l) -> p c l", c=KC)
                dst = out_d[:, t * LT_B:(t + 1) * LT_B].rearrange("(c p) l -> p c l", p=128)
                sO.bump(q.dma_start(out=dst, in_=src), 16)
                ms[('outdma', t)] = sO.n

        HW2 = H * LT_B // 2

        def emit_frontend(t):
            for ocp in range(KC // 2):
                @em.tensor
                def _(T, t=t, ocp=ocp):
                    if ocp == 0:
                        _wk(T, sXB[t % 2], ms, ('xbB', t))
                        if t == 0:
                            _wk(T, sDG, ms, 'wqwo')
                            _wk(T, sV, ms, 'A_V_end')
                    for oc in (2 * ocp, 2 * ocp + 1):
                        if oc >= 4:
                            _wk(T, sS, ms, ('qev', t, ((oc - 4) // 2) * 2 + 1))
                        elif t >= 1:
                            _wk(T, sS, ms, ('yev', t - 1, 4 + 2 * (oc // 2) + 1))
                        pt = ps[(oc // 2) % 2]
                        col = (oc % 2) * LT_B
                        for kc in range(KC):
                            mi = T.matmul(pt[:, col:col + LT_B],
                                          wq_b[:, kc * D + oc * 128: kc * D + (oc + 1) * 128],
                                          xb[:, kc * LT_B:(kc + 1) * LT_B],
                                          start=(kc == 0), stop=(kc == KC - 1))
                        ms[('qmm', t, oc)] = sT.bump(mi)

                @em.scalar
                def _(S, t=t, ocp=ocp):
                    _wk(S, sT, ms, ('qmm', t, 2 * ocp + 1))
                    if ocp == 0 and t == 0:
                        _wk(S, sDL, ms, 'setup_sync')
                    for oc in (2 * ocp, 2 * ocp + 1):
                        col = (oc % 2) * LT_B
                        i = S.activation(u1q[:, oc * LT_B:(oc + 1) * LT_B],
                                         ps[(oc // 2) % 2][:, col:col + LT_B],
                                         AF.Identity, bias=bqc[:, oc:oc + 1])
                        ms[('qev', t, oc)] = sS.bump(i)

            for gp in range(KC // 2):
                @em.tensor
                def _(T, t=t, gp=gp):
                    for grp in (2 * gp, 2 * gp + 1):
                        if grp == 0:
                            _wk(T, sS, ms, ('qev', t, 5))
                        elif grp == 1:
                            _wk(T, sS, ms, ('qev', t, 7))
                        else:
                            _wk(T, sS, ms, ('pqev', t, grp - 2))
                        bank = ps[grp % 2]
                        for h in (2 * grp, 2 * grp + 1):
                            half = h % 2
                            oc = h // 2
                            mi = T.matmul(bank[:, half * LT_B:(half + 1) * LT_B],
                                          projz_b[:, half * 128:(half + 1) * 128],
                                          u1q[:, oc * LT_B:(oc + 1) * LT_B],
                                          start=True, stop=True, skip_group_check=True)
                        ms[('pqmm', t, grp)] = sT.bump(mi)

                @em.scalar
                def _(S, t=t, gp=gp):
                    for grp in (2 * gp, 2 * gp + 1):
                        _wk(S, sT, ms, ('pqmm', t, grp))
                        if grp == 0 and t >= 1:
                            _wk(S, sT, ms, ('omm', t - 1, KC - 1))
                        i = S.activation(p8q[:, grp * 512:(grp + 1) * 512],
                                         ps[grp % 2][:], AF.Identity)
                        ms[('pqev', t, grp)] = sS.bump(i)

        def emit_trig_rr(t, hf):
            @em.vector
            def _(v, t=t, hf=hf):
                _wk(v, sS, ms, ('pqev', t, 4 * hf + 3))
                p8 = p8q[:, hf * HW2:(hf + 1) * HW2]
                v.tensor_single_scalar(tq1[:], p8, -B8, ALU.is_lt)
                v.scalar_tensor_tensor(tq2[:], p8, B8, tq1[:], ALU.is_gt, ALU.subtract)
                v.scalar_tensor_tensor(tq1[:], tq2[:], -C1X8, p8, ALU.mult, ALU.add)
                v.scalar_tensor_tensor(tq2[:], tq2[:], -C2X8, tq1[:], ALU.mult, ALU.add)
                i = v.tensor_scalar(tq1[:], tq2[:], -B8, B8, ALU.max, ALU.min)
                ms[('trq1h', t, hf)] = sV.bump(i)

        def emit_trig_sin(t, hf):
            @em.scalar
            def _(S, t=t, hf=hf):
                _wk(S, sV, ms, ('trq1h', t, hf))
                S.activation(qps[:, hf * HW2:(hf + 1) * HW2], tq1[:], AF.Sin, scale=0.125)
                i = S.activation(tq2[:], tq1[:], AF.Sin, scale=0.0625)
                ms[('shqh', t, hf)] = sS.bump(i)

        def emit_trig_cos(t, hf):
            @em.vector
            def _(v, t=t, hf=hf):
                _wk(v, sS, ms, ('shqh', t, hf))
                v.tensor_tensor(tq2[:], tq2[:], tq2[:], ALU.mult)
                i = v.tensor_scalar(qpc[:, hf * HW2:(hf + 1) * HW2], tq2[:],
                                    -2.0, 1.0, ALU.mult, ALU.add)
                ms[('cosqh', t, hf)] = sV.bump(i)

        def emit_gw_corr(tp, has_next_o):
            for ocp in range(KC // 2):
                @em.tensor
                def _(T, tp=tp, ocp=ocp):
                    if ocp == 0:
                        _wk(T, sV, ms, ('s2a', tp, JC - 1))
                    for oc in (2 * ocp, 2 * ocp + 1):
                        if oc >= 4:
                            _wk(T, sV, ms, ('corr', tp, ((oc - 4) // 2) * 2 + 1))
                        elif has_next_o:
                            _wk(T, sS, ms, ('oevp', tp + 1, 2 + oc // 2))
                        pt = ps[2 + (oc // 2) % 2]
                        col = (oc % 2) * LT_B
                        for jc in range(JC):
                            mi = T.matmul(pt[:, col:col + LT_B],
                                          wpj8[:, jc * D + oc * 128: jc * D + (oc + 1) * 128],
                                          gb[:, jc * LT_B:(jc + 1) * LT_B],
                                          start=(jc == 0), stop=(jc == JC - 1))
                        ms[('gwmm', tp, oc)] = sT.bump(mi)

                @em.vector
                def _(v, tp=tp, ocp=ocp):
                    if ocp == 0:
                        _wk(v, sV, ms, ('ffin', tp))
                    _wk(v, sT, ms, ('gwmm', tp, 2 * ocp + 1))
                    for oc in (2 * ocp, 2 * ocp + 1):
                        col = (oc % 2) * LT_B
                        v.tensor_scalar(t1b[:], mrs[:], ucol[:, oc:oc + 1], ncol[:, oc:oc + 1],
                                        ALU.mult, ALU.add)
                        v.tensor_tensor(t2b[:], ps[2 + (oc // 2) % 2][:, col:col + LT_B],
                                        rsbF[:], ALU.mult)
                        v.scalar_tensor_tensor(t2b[:], t1b[:], -1.0, t2b[:], ALU.mult, ALU.add)
                        sl = zt[tp % 2][:, oc * LT_B:(oc + 1) * LT_B]
                        i = v.tensor_tensor(sl, t2b[:],
                                            x1b[tp % 2][:, oc * LT_B:(oc + 1) * LT_B], ALU.add)
                        ms[('corr', tp, oc)] = sV.bump(i)

        emit_frontend(0)
        for hf in range(2):
            emit_trig_rr(0, hf)
            emit_trig_sin(0, hf)
            emit_trig_cos(0, hf)
        emit_xb_load(1)

        for t in range(NTB):
            ztc = zt[t % 2]
            ztp = zt[(t - 1) % 2]

            if t + 1 < NTB:
                @em.sync
                def _(q, t=t):
                    _wk(q, sV, ms, ('zadd', t))
                    src = x_d[:, (t + 1) * LT_B:(t + 2) * LT_B].rearrange("(c p) l -> p c l", p=128)
                    dst = xf[:].rearrange("p (c l) -> p c l", c=KC)
                    sXF[(t + 1) % 2].bump(q.dma_start(out=dst, in_=src), 16)
                    ms[('xf', t + 1)] = sXF[(t + 1) % 2].n

            for pp in range(KC // 2):
                @em.tensor
                def _(T, t=t, pp=pp):
                    if pp == 0:
                        _wk(T, sV, ms, ('cosqh', t, 1))
                        if t == 0:
                            _wk(T, sS, ms, ('kvtev', 2 * KC - 1))
                    for pair in (2 * pp, 2 * pp + 1):
                        if pair % 2 == 0:
                            if pair >= 4:
                                _wk(T, sS, ms, ('oevp', t, pp - 2))
                            elif t >= 2:
                                _wk(T, sV, ms, ('corr', t - 2, 5 + pair))
                        bank = ps[2 + (pair // 2) % 2]
                        col = (pair % 2) * LT_B
                        for sub in range(2):
                            h = 2 * pair + sub
                            tp = (0, 0) if sub == 0 else (0, 64)
                            T.matmul(bank[sub * 64:(sub + 1) * 64, col:col + LT_B],
                                     kvT[:, (h * 2) * 64:(h * 2 + 1) * 64],
                                     qpc[:, h * LT_B:(h + 1) * LT_B],
                                     start=True, stop=False, tile_position=tp, skip_group_check=True)
                            mi = T.matmul(bank[sub * 64:(sub + 1) * 64, col:col + LT_B],
                                          kvT[:, (h * 2 + 1) * 64:(h * 2 + 2) * 64],
                                          qps[:, h * LT_B:(h + 1) * LT_B],
                                          start=False, stop=True, tile_position=tp, skip_group_check=True)
                    ms[('omm', t, 2 * pp + 1)] = sT.bump(mi)

                @em.scalar
                def _(S, t=t, pp=pp):
                    _wk(S, sT, ms, ('omm', t, 2 * pp + 1))
                    i = S.activation(u1o[:, 2 * pp * LT_B:(2 * pp + 2) * LT_B],
                                     ps[2 + pp % 2][:, 0:2 * LT_B], AF.Identity)
                    ms[('oevp', t, pp)] = sS.bump(i)

            for ocp in range(KC // 2):
                @em.tensor
                def _(T, t=t, ocp=ocp):
                    if ocp == 0:
                        _wk(T, sS, ms, ('oevp', t, KC // 2 - 1))
                        if t >= 2:
                            _wk(T, sV, ms, ('l2var', t - 2))
                    for oc in (2 * ocp, 2 * ocp + 1):
                        if oc >= 4:
                            _wk(T, sS, ms, ('yev', t, ((oc - 4) // 2) * 2 + 1))
                        elif oc < 4:
                            _wk(T, sS, ms, ('pqev', t, 6 + (oc // 2) % 2))
                        pt = ps[(oc // 2) % 2]
                        col = (oc % 2) * LT_B
                        for kc in range(KC):
                            mi = T.matmul(pt[:, col:col + LT_B],
                                          wo_b[:, kc * D + oc * 128: kc * D + (oc + 1) * 128],
                                          u1o[:, kc * LT_B:(kc + 1) * LT_B],
                                          start=(kc == 0), stop=(kc == KC - 1))
                        ms[('ymm', t, oc)] = sT.bump(mi)

                @em.scalar
                def _(S, t=t, ocp=ocp, ztc=ztc):
                    _wk(S, sT, ms, ('ymm', t, 2 * ocp + 1))
                    if ocp == 0 and t >= 2:
                        _wk(S, sO, ms, ('outdma', t - 2))
                    for oc in (2 * ocp, 2 * ocp + 1):
                        col = (oc % 2) * LT_B
                        i = S.activation(ztc[:, oc * LT_B:(oc + 1) * LT_B],
                                         ps[(oc // 2) % 2][:, col:col + LT_B],
                                         AF.Identity, bias=boc[:, oc:oc + 1])
                        ms[('yev', t, oc)] = sS.bump(i)

            if t + 1 < NTB:
                emit_frontend(t + 1)
                if t + 2 < NTB:
                    emit_xb_load(t + 2)

            @em.vector
            def _(v, t=t, ztc=ztc):
                _wk(v, sS, ms, ('yev', t, KC - 1))
                _wk(v, sXF[t % 2], ms, ('xf', t))
                for oc in range(KC):
                    sl = ztc[:, oc * LT_B:(oc + 1) * LT_B]
                    mi = v.tensor_tensor(sl, sl, xf[:, oc * LT_B:(oc + 1) * LT_B], ALU.add)
                ms[('zadd', t)] = sV.bump(mi)

            emit_ln_tree('l1', t, ztc, u1o, ('zadd', t))
            emit_ln_smm('l1', t)
            emit_ln_var('l1', t)
            emit_ln_sd('l1', t)
            emit_ln_fin('l1', t, ztc)
            if t + 1 < NTB:
                emit_trig_rr(t + 1, 0)
                emit_trig_sin(t + 1, 0)

            if t >= 1:
                emit_gw_corr(t - 1, True)
            if t + 1 < NTB:
                emit_trig_cos(t + 1, 0)
            if t >= 1:
                emit_ln_tree('l2', t - 1, ztp, u1q, ('corr', t - 1, KC - 1))
                gks = ((('pqev', t + 1, 6), ('pqev', t + 1, 7)) if t + 1 < NTB
                       else (('yev', t, 5), ('yev', t, 7)))
                emit_ln_smm('l2', t - 1, gate_keys=gks)
                emit_ln_var('l2', t - 1)
            if t + 1 < NTB:
                emit_trig_rr(t + 1, 1)
            if t >= 1:
                emit_ln_fin('l2', t - 1, ztp)
                emit_out_dma(t - 1)

            def emit_stat_mms(T, t, jc):
                st_, sp_ = (jc == 0), (jc == JC - 1)
                _wk(T, sS, ms, ('gqv', t, jc // 2))
                T.matmul(psS[:, 0:LT_B], ones_b[:], gb[:, jc * LT_B:(jc + 1) * LT_B],
                         start=st_, stop=sp_, skip_group_check=True)
                mi = T.matmul(psS2[:, 0:LT_B], ones_b[:],
                              gq[:, (jc % 4) * LT_B:(jc % 4 + 1) * LT_B],
                              start=st_, stop=sp_, skip_group_check=True)
                ms[('ssmm', t, jc)] = sT.bump(mi)

            for jcp in range(JC // 2):
                @em.tensor
                def _(T, t=t, jcp=jcp):
                    if jcp == 0:
                        _wk(T, sV, ms, ('l1fin', t))
                        if t >= 1:
                            _wk(T, sS, ms, ('gevp', t - 1, 14))
                    if jcp == 1 and t >= 1:
                        _wk(T, sS, ms, ('gevp', t - 1, 15))
                    for jc in (2 * jcp, 2 * jcp + 1):
                        if jc >= 4:
                            _wk(T, sS, ms, ('gevp', t, (jc - 4) // 2))
                        pt = ps[4 + (jc // 2) % 2]
                        col = (jc % 2) * LT_B
                        for kc in range(KC):
                            mi = T.matmul(pt[:, col:col + LT_B],
                                          wfc8[:, kc * 4096 + jc * 128: kc * 4096 + (jc + 1) * 128],
                                          x1b[t % 2][:, kc * LT_B:(kc + 1) * LT_B],
                                          start=(kc == 0), stop=(kc == KC - 1))
                        ms[('hmm', t, jc)] = sT.bump(mi)
                    if jcp >= 1:
                        for jc in (2 * (jcp - 1), 2 * (jcp - 1) + 1):
                            emit_stat_mms(T, t, jc)

                @em.scalar
                def _(S, t=t, jcp=jcp):
                    jc0 = 2 * jcp
                    _wk(S, sT, ms, ('hmm', t, jc0 + 1))
                    if jcp == 0 and t >= 1:
                        _wk(S, sT, ms, ('gwmm', t - 1, KC - 1))
                    if jcp >= 2:
                        _wk(S, sT, ms, ('ssmm', t, jc0 - 4 + 1))
                    pt = ps[4 + jcp % 2]
                    i = S.activation(gb[:, jc0 * LT_B:(jc0 + 2) * LT_B], pt[:, 0:2 * LT_B], AF.Gelu)
                    ms[('gevp', t, jcp)] = sS.bump(i)
                    gpair = gb[:, jc0 * LT_B:(jc0 + 2) * LT_B]
                    i2 = S.activation(gq[:, (jc0 % 4) * LT_B:(jc0 % 4 + 2) * LT_B],
                                      gpair, AF.Square)
                    ms[('gqv', t, jcp)] = sS.bump(i2)

                if jcp >= 1:
                    @em.vector
                    def _(v, t=t, jcp=jcp):
                        for jc in (2 * (jcp - 1), 2 * (jcp - 1) + 1):
                            _wk(v, sT, ms, ('ssmm', t, jc))
                            gsl = gb[:, jc * LT_B:(jc + 1) * LT_B]
                            i = v.tensor_scalar(gsl, gsl, lnwf[:, jc:jc + 1], None,
                                                ALU.mult, ALU.bypass)
                            ms[('s2a', t, jc)] = sV.bump(i)
                if jcp == 4 and t >= 1:
                    emit_ln_sd('l2', t - 1)
                if t + 1 < NTB:
                    if jcp == 11:
                        emit_trig_sin(t + 1, 1)
                    elif jcp == 14:
                        emit_trig_cos(t + 1, 1)

            @em.tensor
            def _(T, t=t):
                for jc in (JC - 2, JC - 1):
                    emit_stat_mms(T, t, jc)

            @em.vector
            def _(v, t=t):
                for jc in (JC - 2, JC - 1):
                    _wk(v, sT, ms, ('ssmm', t, jc))
                    gsl = gb[:, jc * LT_B:(jc + 1) * LT_B]
                    i = v.tensor_scalar(gsl, gsl, lnwf[:, jc:jc + 1], None, ALU.mult, ALU.bypass)
                    ms[('s2a', t, jc)] = sV.bump(i)


            @em.vector
            def _(v, t=t):
                _wk(v, sT, ms, ('ssmm', t, JC - 1))
                v.tensor_single_scalar(m1[:], psS[:, 0:LT_B], float(2.0 ** -12), ALU.mult)
                v.tensor_tensor(m2[:], m1[:], m1[:], ALU.mult)
                i = v.scalar_tensor_tensor(m2[:], psS2[:, 0:LT_B], float(2.0 ** -12),
                                           m2[:], ALU.mult, ALU.subtract)
                ms[('fvar', t)] = sV.bump(i)

            @em.scalar
            def _(S, t=t):
                _wk(S, sV, ms, ('fvar', t))
                i = S.activation(m2[:], m2[:], AF.Sqrt, bias=eps5[:, 0:1])
                ms[('fsd', t)] = sS.bump(i)

            @em.vector
            def _(v, t=t):
                _wk(v, sS, ms, ('fsd', t))
                ri = v.reciprocal(rsbF[:], m2[:])
                _w(v, sV, sV.bump(ri))
                i = v.tensor_tensor(mrs[:], m1[:], rsbF[:], ALU.mult)
                ms[('ffin', t)] = sV.bump(i)

            em.flush()

        tl = NTB - 1
        emit_gw_corr(tl, False)
        emit_ln_tree('l2', tl, zt[tl % 2], u1q, ('corr', tl, KC - 1))
        emit_ln_smm('l2', tl, gate_keys=(('yev', tl, 5), ('yev', tl, 7)))
        emit_ln_var('l2', tl)
        emit_ln_sd('l2', tl)
        emit_ln_fin('l2', tl, zt[tl % 2])
        emit_out_dma(tl)

        @em.sync
        def _(q):
            q.wait_ge(sO.h, sO.n)
        em.flush()

        Bk.close()
        I.close()

    return nc


def kernel(L0_attn_x, L0_attn_proj_mat, L0_attn_Wq, L0_attn_bq, L0_attn_Wk,
           L0_attn_bk, L0_attn_Wv, L0_attn_bv, L0_attn_Wo, L0_attn_bo,
           L0_ffn_W_fc, L0_ffn_ln_w, L0_ffn_ln_b, L0_ffn_W_proj):
    x = np.asarray(L0_attn_x, dtype=np.float32)
    f32 = lambda a: np.ascontiguousarray(np.asarray(a, dtype=np.float32))
    proj = f32(L0_attn_proj_mat)
    shared = {
        "wqT": f32(np.asarray(L0_attn_Wq, np.float32).T),
        "wkT": f32(np.asarray(L0_attn_Wk, np.float32).T),
        "wvT": f32(np.asarray(L0_attn_Wv, np.float32).T),
        "woT": f32(np.asarray(L0_attn_Wo, np.float32).T),
        "wfcT": f32(np.asarray(L0_ffn_W_fc, np.float32).T),
        "wpjT": f32(np.asarray(L0_ffn_W_proj, np.float32).T),
        "projz": f32(np.hstack([np.vstack([proj, np.zeros((64, 128), np.float32)]),
                                np.vstack([np.zeros((64, 128), np.float32), proj])])),
        "eye": np.eye(128, dtype=np.float32),
        "ones": np.ones((128, 128), np.float32),
        "bqc": f32(np.asarray(L0_attn_bq, np.float32).reshape(8, 128).T),
        "bkc": f32(np.asarray(L0_attn_bk, np.float32).reshape(8, 128).T),
        "boc": f32(np.asarray(L0_attn_bo, np.float32).reshape(8, 128).T),
        "bvb": f32(np.broadcast_to(np.asarray(L0_attn_bv, np.float32), (128, D))),
        "lnwc": f32(np.asarray(L0_ffn_ln_w, np.float32).reshape(32, 128).T),
        "lnbc": f32(np.asarray(L0_ffn_ln_b, np.float32).reshape(32, 128).T),
    }
    nc = build_nc()
    in_maps = []
    for b in range(8):
        m = dict(shared)
        m["xT"] = np.ascontiguousarray(x[b].T)
        in_maps.append(m)
    res = run_bass_kernel_spmd(nc, in_maps, core_ids=list(range(8)))
    out = np.empty((8, L, D), np.float32)
    for b in range(8):
        out[b] = res.results[b]["outT"].T
    return out
